# version 26
# baseline (speedup 1.0000x reference)
import numpy as np
import ml_dtypes
import concourse.bass as bass
import concourse.mybir as mybir
from concourse.bass_utils import run_bass_kernel_spmd

F32 = mybir.dt.float32
BF16 = mybir.dt.bfloat16
AF = mybir.ActivationFunctionType
ALU = mybir.AluOpType
AX = mybir.AxisListType
BF = ml_dtypes.bfloat16

NCORES = 8
H = 4096
II = 11008
TCORE = 1024
NPASS = 2
NT = 4
NHC = 32
NM = 11
MW = [1024] * 10 + [768]
NIC = 86
NOPAIR = 4
SUB_ITERS = 6
FULL_ITERS = 5
NWBUF = 8
NDBUF = 8

C_SX, C_SX2, C_M, C_V, C_RSTD, C_T1, C_TMID, C_NTM, C_NT2, C_DIF, C_S, C_MSK, \
    C_Q, C_D1, C_D2, C_THR, C_SC0, C_SC1, C_SC2, C_SC3, C_SC4, C_SC5, C_SC6, C_SC7 = range(24)
C_GM, C_GS, C_GR = 24, 25, 26
C_G4 = 28
C_GE4 = 32
C_MJ = 36
C_RJ = 40
C_TMJ = 44
C_NMJ = 48
NCOLS = 52
SUB_N = 1376
SEC_HW = 2.0 / 64 / 16
SEC_CL = 2.0 / 64 / 2


def build_nc():
    nc = bass.Bass("TRN2", target_bir_lowering=False, debug=False)
    xd = nc.dram_tensor("x", [TCORE, H], F32, kind="ExternalInput").ap()
    gtd = nc.dram_tensor("gt", [H, II], BF16, kind="ExternalInput").ap()
    utd = nc.dram_tensor("ut", [H, II], BF16, kind="ExternalInput").ap()
    dtd = nc.dram_tensor("dt", [II + H, H], BF16, kind="ExternalInput").ap()
    a2d = nc.dram_tensor("a2", [128, NHC, 32], BF16, kind="ExternalInput").ap()
    wrtd = nc.dram_tensor("wrt", [128, NHC, 4], BF16, kind="ExternalInput").ap()
    bmtd = nc.dram_tensor("bmt", [128, H], BF16, kind="ExternalInput").ap()
    bpd = nc.dram_tensor("bpad", [128, H], BF16, kind="ExternalInput").ap()
    e0d = nc.dram_tensor("e0", [128, 128], BF16, kind="ExternalInput").ap()
    idd = nc.dram_tensor("ident", [128, 128], BF16, kind="ExternalInput").ap()
    yd = nc.dram_tensor("y", [TCORE, H], F32, kind="ExternalOutput").ap()

    plan = {k: [] for k in ("sync", "scalar", "vector", "tensor", "gpsimd")}

    def E(eng, fn):
        plan[eng].append(fn)

    from contextlib import ExitStack
    with ExitStack() as ctx:
        SM = ctx.enter_context(nc.sbuf_tensor("SM", [128, NT, II], BF16))
        XNF = ctx.enter_context(nc.sbuf_tensor("XNF", [128, NHC, 512], BF16))
        ARN = ctx.enter_context(nc.sbuf_tensor("ARN", [128, 5504], F32))
        WSTR = ctx.enter_context(nc.sbuf_tensor("WSTR", [128, NWBUF, 1024], BF16))
        DSTR = ctx.enter_context(nc.sbuf_tensor("DSTR", [128, NDBUF, 1024], BF16))
        YST = ctx.enter_context(nc.sbuf_tensor("YST", [128, 2, 512], F32))
        XRS = ctx.enter_context(nc.sbuf_tensor("XRS", [128, 2, 512], F32))
        JNKB = ctx.enter_context(nc.sbuf_tensor("JNKB", [128, H], BF16))
        QF = ctx.enter_context(nc.sbuf_tensor("QF", [128, 512], BF16))
        QT = ctx.enter_context(nc.sbuf_tensor("QT", [128, 32], BF16))
        G2T = ctx.enter_context(nc.sbuf_tensor("G2T", [128, 4], BF16))
        A2S = ctx.enter_context(nc.sbuf_tensor("A2S", [128, NHC, 32], BF16))
        WRTS = ctx.enter_context(nc.sbuf_tensor("WRTS", [128, NHC, 4], BF16))
        BMTS = ctx.enter_context(nc.sbuf_tensor("BMTS", [128, H], BF16))
        BPADS = ctx.enter_context(nc.sbuf_tensor("BPADS", [128, H], BF16))
        E0S = ctx.enter_context(nc.sbuf_tensor("E0S", [128, 128], BF16))
        IDENT = ctx.enter_context(nc.sbuf_tensor("IDENT", [128, 128], BF16))
        COLS = ctx.enter_context(nc.sbuf_tensor("COLS", [128, NCOLS], F32))
        PSA = ctx.enter_context(nc.psum_tensor("PSA", [128, 4096], F32))
        PS = [PSA[:, 512 * i:512 * (i + 1)] for i in range(8)]
        s_pe = ctx.enter_context(nc.semaphore("s_pe"))
        s_ve = ctx.enter_context(nc.semaphore("s_ve"))
        s_se = ctx.enter_context(nc.semaphore("s_se"))
        s_ws = [ctx.enter_context(nc.semaphore(f"s_ws{i}")) for i in range(4)]
        s_wg = [ctx.enter_context(nc.semaphore(f"s_wg{i}")) for i in range(4)]
        s_wp = [ctx.enter_context(nc.semaphore(f"s_wp{i}")) for i in range(4)]
        s_ds = [ctx.enter_context(nc.semaphore(f"s_ds{i}")) for i in range(NDBUF)]
        s_xl = ctx.enter_context(nc.semaphore("s_xl"))
        s_xr = [ctx.enter_context(nc.semaphore(f"s_xr{i}")) for i in range(2)]
        s_yy = [ctx.enter_context(nc.semaphore(f"s_yy{i}")) for i in range(2)]
        s_cst = ctx.enter_context(nc.semaphore("s_cst"))
        blk = ctx.enter_context(nc.Block())
        SEM = {"pe": s_pe, "ve": s_ve, "se": s_se}
        ABSV = ARN[:].bitcast(BF16)[:, 0:II]
        XT = ARN[:, 0:H]

        cnt = {"pe": 0, "ve": 0, "se": 0}
        wcnt = [0] * NWBUF
        wpcnt = [0, 0, 0, 0]
        dcnt = [0] * NDBUF
        xlcnt = [0]
        xrcnt = [0, 0]
        yycnt = [0, 0]
        bank_war = [("ve", 0)] * 8
        wstr_war = [0] * NWBUF
        dstr_war = [0] * NDBUF
        widx = [0]
        didx = [0]
        ms_xnt = {}
        ms_mask = {}
        ms_smdone = {}
        ms_qevac = {}
        ms_lntrans = {}
        ms_yevac = {}
        ms_b1evac_last = [0]

        def col(i, n=1):
            return COLS[:, i:i + n]

        NCST = 6
        def pre_sy(e):
            e.dma_start(out=IDENT[:], in_=idd).then_inc(s_cst, 16)
            e.dma_start(out=E0S[:], in_=e0d).then_inc(s_cst, 16)
            e.dma_start(out=A2S[:], in_=a2d).then_inc(s_cst, 16)
            e.dma_start(out=WRTS[:], in_=wrtd).then_inc(s_cst, 16)
            e.dma_start(out=BMTS[:], in_=bmtd).then_inc(s_cst, 16)
            e.dma_start(out=BPADS[:], in_=bpd).then_inc(s_cst, 16)
        E("sync", pre_sy)

        cnt["ve"] += 1
        def _pre_v(e):
            e.memset(QF[:], 0.0)
            e.drain().then_inc(s_ve, 1)
        E("vector", _pre_v)
        E("tensor", lambda e: e.wait_ge(s_cst, 16 * NCST))

        stats_ms = {}
        var_ms = {}
        rstd_ms = {}

        def emit_lnA(p):
            for j in range(NT):
                row = 512 * p + 128 * j
                if p == 0 and j == 0:
                    wv = None
                elif j == 0:
                    wv = ("ve", ms_mask[(p - 1, NT - 1)])
                else:
                    wv = ("se", stats_ms[(p, j - 1)])
                xlcnt[0] += 1
                xc = xlcnt[0]
                def f(e, wv=wv, row=row):
                    if wv:
                        e.wait_ge(SEM[wv[0]], wv[1])
                    e.dma_start(out=XT, in_=xd[row:row + 128, :]).then_inc(s_xl, 16)
                E("gpsimd", f)

                cnt["se"] += 1
                stats_ms[(p, j)] = cnt["se"]
                def f(e, xc=xc, vw=(var_ms.get((p, j - 1), 0))):
                    e.wait_ge(s_xl, 16 * xc)
                    if vw:
                        e.wait_ge(s_ve, vw)
                    e.activation(out=JNKB[:], in_=XT, func=AF.Identity,
                                 accum_out=col(C_SX))
                    e.drain()
                    e.activation(out=JNKB[:], in_=XT, func=AF.Square,
                                 accum_out=col(C_SX2))
                    e.drain().then_inc(s_se, 1)
                E("scalar", f)

                cnt["ve"] += 1
                var_ms[(p, j)] = cnt["ve"]
                def f(e, j=j, sm_=stats_ms[(p, j)]):
                    e.wait_ge(s_se, sm_)
                    e.tensor_scalar(out=col(C_MJ + j), in0=col(C_SX), scalar1=1.0 / H,
                                    scalar2=None, op0=ALU.mult)
                    e.tensor_scalar(out=col(C_V), in0=col(C_SX2), scalar1=1.0 / H,
                                    scalar2=None, op0=ALU.mult)
                    e.drain()
                    e.tensor_tensor(out=col(C_T1), in0=col(C_MJ + j),
                                    in1=col(C_MJ + j), op=ALU.mult)
                    e.drain()
                    e.tensor_tensor(out=col(C_V), in0=col(C_V), in1=col(C_T1),
                                    op=ALU.subtract)
                    e.drain()
                    e.tensor_scalar(out=col(C_V), in0=col(C_V), scalar1=1e-5,
                                    scalar2=None, op0=ALU.add)
                    e.drain().then_inc(s_ve, 1)
                E("vector", f)

                cnt["se"] += 1
                ln_ms = cnt["se"]
                def f(e, vm=var_ms[(p, j)]):
                    e.wait_ge(s_ve, vm)
                    e.activation(out=col(C_T1), in_=col(C_V), func=AF.Ln)
                    e.drain().then_inc(s_se, 1)
                E("scalar", f)
                cnt["ve"] += 1
                lnh_ms = cnt["ve"]
                def f(e, lm=ln_ms):
                    e.wait_ge(s_se, lm)
                    e.tensor_scalar(out=col(C_T1), in0=col(C_T1), scalar1=-0.5,
                                    scalar2=None, op0=ALU.mult)
                    e.drain().then_inc(s_ve, 1)
                E("vector", f)
                cnt["se"] += 1
                rstd_ms[(p, j)] = cnt["se"]
                def f(e, j=j, lh=lnh_ms):
                    e.wait_ge(s_ve, lh)
                    e.activation(out=col(C_RJ + j), in_=col(C_T1), func=AF.Exp)
                    e.drain().then_inc(s_se, 1)
                E("scalar", f)

        for p in range(NPASS):
            if p == 0:
                emit_lnA(0)
            for j in range(NT):
                row = 512 * p + 128 * j

                if j == 0:
                    wv = ("se", stats_ms[(p, NT - 1)])
                else:
                    wv = ("ve", ms_xnt[(p, j - 1)])
                xlcnt[0] += 1
                xc = xlcnt[0]
                def f(e, wv=wv, row=row):
                    e.wait_ge(SEM[wv[0]], wv[1])
                    e.dma_start(out=XT, in_=xd[row:row + 128, :]).then_inc(s_xl, 16)
                E("gpsimd", f)

                jw = ms_lntrans.get((p, j - 1), ms_lntrans.get((p - 1, NT - 1), 0))
                cnt["ve"] += 1
                ms_xnt[(p, j)] = cnt["ve"]
                def f(e, j=j, xc=xc, jw=jw, rm=rstd_ms[(p, j)]):
                    e.wait_ge(s_xl, 16 * xc)
                    e.wait_ge(s_se, rm)
                    if jw:
                        e.wait_ge(s_pe, jw)
                    e.tensor_scalar(out=JNKB[:], in0=XT, scalar1=col(C_MJ + j),
                                    scalar2=col(C_RJ + j), op0=ALU.subtract,
                                    op1=ALU.mult)
                    e.drain().then_inc(s_ve, 1)
                E("vector", f)

                for g8 in range(8):
                    bank = g8 % 2
                    bw = bank_war[bank]
                    cnt["pe"] += 1
                    grp_ms = cnt["pe"]
                    def f(e, g8=g8, bank=bank, bw=bw, j=j, first=(g8 == 0),
                          xnt=ms_xnt[(p, j)]):
                        if first:
                            e.wait_ge(s_ve, xnt)
                        e.wait_ge(SEM[bw[0]], bw[1])
                        for k in range(4):
                            hc = 4 * g8 + k
                            mm = e.matmul(PS[bank][:, 128 * k:128 * k + 128],
                                          JNKB[:, 128 * hc:128 * hc + 128], IDENT[:],
                                          start=True, stop=True)
                        mm.then_inc(s_pe, 1)
                    E("tensor", f)
                    cnt["ve"] += 1
                    ev_ms = cnt["ve"]
                    bank_war[bank] = ("ve", ev_ms)
                    def f(e, g8=g8, bank=bank, j=j, grp_ms=grp_ms):
                        e.wait_ge(s_pe, grp_ms)
                        e.tensor_copy(out=XNF[:, 4 * g8:4 * g8 + 4, 128 * j:128 * j + 128],
                                      in_=PS[bank][:, 0:512])
                        e.drain().then_inc(s_ve, 1)
                    E("vector", f)
                ms_lntrans[(p, j)] = cnt["pe"]
                xnf_done = cnt["ve"]

                bw2, bw3 = bank_war[2], bank_war[3]
                cnt["pe"] += 2
                gat_ms = cnt["pe"] - 1
                pt_ms = cnt["pe"]
                def f(e, j=j, bw2=bw2, bw3=bw3, xnf_done=xnf_done):
                    e.wait_ge(s_ve, xnf_done)
                    e.wait_ge(SEM[bw2[0]], bw2[1])
                    for hc in range(NHC):
                        mm = e.matmul(PS[2][:, 0:4], XNF[:, hc, 128 * j:128 * j + 128],
                                      WRTS[:, hc, :], start=(hc == 0), stop=(hc == NHC - 1),
                                      skip_group_check=True)
                    mm.then_inc(s_pe, 1)
                    e.wait_ge(SEM[bw3[0]], bw3[1])
                    for hc in range(NHC):
                        mm = e.matmul(PS[3][:, 0:32], XNF[:, hc, 128 * j:128 * j + 128],
                                      A2S[:, hc, :], start=(hc == 0), stop=(hc == NHC - 1),
                                      skip_group_check=True)
                    mm.then_inc(s_pe, 1)
                E("tensor", f)

                cnt["ve"] += 1
                gsub_ms = cnt["ve"]
                def f(e, gat_ms=gat_ms):
                    e.wait_ge(s_pe, gat_ms)
                    e.tensor_copy(out=col(C_G4, 4), in_=PS[2][:, 0:4])
                    e.drain()
                    e.reduce_max(out=col(C_GM), in_=col(C_G4, 4), axis=AX.X)
                    e.drain()
                    e.tensor_scalar(out=col(C_G4, 4), in0=col(C_G4, 4), scalar1=col(C_GM),
                                    scalar2=None, op0=ALU.subtract)
                    e.drain().then_inc(s_ve, 1)
                E("vector", f)
                bank_war[2] = ("ve", gsub_ms)

                cnt["se"] += 1
                gexp_ms = cnt["se"]
                def f(e, gsub_ms=gsub_ms):
                    e.wait_ge(s_ve, gsub_ms)
                    e.activation(out=col(C_GE4, 4), in_=col(C_G4, 4), func=AF.Exp)
                    e.drain().then_inc(s_se, 1)
                E("scalar", f)

                cnt["ve"] += 1
                qt_ms = cnt["ve"]
                bank_war[3] = ("ve", qt_ms)
                def f(e, gexp_ms=gexp_ms, pt_ms=pt_ms):
                    e.wait_ge(s_se, gexp_ms)
                    e.reduce_sum(out=col(C_GS), in_=col(C_GE4, 4), axis=AX.X)
                    e.drain()
                    e.reciprocal(out=col(C_GR), in_=col(C_GS))
                    e.drain()
                    e.tensor_scalar(out=G2T[:], in0=col(C_GE4, 4), scalar1=col(C_GR),
                                    scalar2=2.0, op0=ALU.mult, op1=ALU.mult)
                    e.drain()
                    e.wait_ge(s_pe, pt_ms)
                    e.tensor_tensor(out=QT[:].rearrange("p (e r) -> p e r", r=8),
                                    in0=PS[3][:, 0:32].rearrange("p (e r) -> p e r", r=8),
                                    in1=G2T[:].unsqueeze(2).broadcast_to([128, 4, 8]),
                                    op=ALU.mult)
                    e.drain().then_inc(s_ve, 1)
                E("vector", f)

                bw4 = bank_war[4]
                cnt["pe"] += 1
                qtr_ms = cnt["pe"]
                def f(e, j=j, bw4=bw4, qt_ms=qt_ms):
                    e.wait_ge(s_ve, qt_ms)
                    e.wait_ge(SEM[bw4[0]], bw4[1])
                    e.matmul(PS[4][0:32, 128 * j:128 * j + 128], QT[:], IDENT[:],
                             start=True, stop=True).then_inc(s_pe, 1)
                E("tensor", f)
                cnt["ve"] += 1
                ms_qevac[(p, j)] = cnt["ve"]
                bank_war[4] = ("ve", cnt["ve"])
                def f(e, j=j, qtr_ms=qtr_ms):
                    e.wait_ge(s_pe, qtr_ms)
                    e.tensor_copy(out=QF[0:32, 128 * j:128 * j + 128],
                                  in_=PS[4][0:32, 128 * j:128 * j + 128])
                    e.drain().then_inc(s_ve, 1)
                E("vector", f)

            for m in range(NM):
                w = MW[m]
                i0 = 1024 * m
                slices = [(0, 512), (512, w - 512)]
                m_first_waits = [bank_war[4 * si + j] for si in range(2) for j in range(NT)]
                pair_sem = {}
                for hc in range(NHC):
                    if hc % 2 == 0:
                        hc2 = hc // 2
                        widx[0] += 1
                        qgp = widx[0] % 2
                        b0 = 4 * qgp + 2 * ((widx[0] // 2) % 2)
                        sem = s_wg[b0 - 4] if qgp else s_ws[b0]
                        war = max(wstr_war[b0], wstr_war[b0 + 1])
                        wcnt[b0] += 1
                        gc = wcnt[b0]
                        pair_sem[hc2] = (sem, gc, b0)
                        def f(e, war=war, b0=b0, hcp=hc, i0=i0, w=w, sem=sem):
                            if war:
                                e.wait_ge(s_pe, war)
                            e.dma_start(out=WSTR[:, b0:b0 + 2, 0:w],
                                        in_=gtd[128 * hcp:128 * hcp + 256, i0:i0 + w]
                                        .rearrange("(c p) w -> p c w", p=128)).then_inc(sem, 16)
                        E("gpsimd" if qgp else "sync", f)
                    sem, gc, b0 = pair_sem[hc // 2]
                    buf = b0 + hc % 2
                    cnt["pe"] += 1
                    pe_ms = cnt["pe"]
                    wstr_war[buf] = pe_ms
                    def f(e, gc=gc, buf=buf, hc=hc, slices=slices, sem=sem,
                          first=(hc == 0), waits=m_first_waits):
                        e.wait_ge(sem, 16 * gc)
                        if first:
                            for bw in waits:
                                e.wait_ge(SEM[bw[0]], bw[1])
                        for si, (off, wsl) in enumerate(slices):
                            for j in range(NT):
                                mm = e.matmul(PS[4 * si + j][:, 0:wsl],
                                              XNF[:, hc, 128 * j:128 * j + 128],
                                              WSTR[:, buf, off:off + wsl],
                                              start=(hc == 0), stop=(hc == NHC - 1),
                                              skip_group_check=True)
                        mm.then_inc(s_pe, 1)
                    E("tensor", f)
                m_last_pe = cnt["pe"]

                for si, (off, wsl) in enumerate(slices):
                    cnt["se"] += 1
                    ev = cnt["se"]
                    for j in range(NT):
                        bank_war[4 * si + j] = ("se", ev)
                    def f(e, si=si, off=off, wsl=wsl, i0=i0,
                          m_last_pe=m_last_pe, first=(si == 0)):
                        if first:
                            e.wait_ge(s_pe, m_last_pe)
                        e.activation(out=SM[:, :, i0 + off:i0 + off + wsl],
                                     in_=PSA[:, 2048 * si:2048 * si + 2048]
                                     .rearrange("p (j w) -> p j w", w=512)[:, :, 0:wsl],
                                     func=AF.Silu)
                        e.drain().then_inc(s_se, 1)
                    E("scalar", f)
                if m == NM - 1:
                    for j in range(NT):
                        ms_smdone[(p, j)] = cnt["se"]

            DSAMP = DSTR[:].rearrange("p a b -> p (a b)")
            cnt["ve"] += 1
            samp_ms = cnt["ve"]
            def f(e, sds=[(i, dcnt[i]) for i in range(NDBUF)],
                  sdone=[ms_smdone[(p, j)] for j in range(NT)]):
                e.wait_ge(s_se, max(sdone))
                for i, c in sds:
                    if c:
                        e.wait_ge(s_ds[i], 16 * c)
                for j in range(NT):
                    e.scalar_tensor_tensor(out=DSAMP[:, 1376 * j:1376 * (j + 1)],
                                           in0=SM[:, j, 0:SUB_N], scalar=-1.0,
                                           in1=SM[:, j, 0:SUB_N],
                                           op0=ALU.mult, op1=ALU.max)
                    e.drain()
                e.memset(col(C_TMJ, 4), 1.0)
                e.memset(col(C_NMJ, 4), -1.0)
                e.drain().then_inc(s_ve, 1)
            E("vector", f)

            for it in range(SUB_ITERS):
                w_it = 2.0 / (2 ** (it + 1))
                cnt["se"] += 1
                s_it = cnt["se"]
                def f(e, vw=(cnt["ve"] if it else samp_ms)):
                    e.wait_ge(s_ve, vw)
                    for j in range(NT):
                        if j:
                            e.drain()
                        e.activation(out=JNKB[:, 0:SUB_N],
                                     in_=DSAMP[:, 1376 * j:1376 * (j + 1)],
                                     func=AF.Sign, bias=col(C_NMJ + j),
                                     accum_out=col(C_SC0 + j))
                    e.drain().then_inc(s_se, 1)
                E("scalar", f)
                cnt["ve"] += 1
                def f(e, s_it=s_it, w_it=w_it):
                    e.wait_ge(s_se, s_it)
                    e.tensor_single_scalar(out=col(C_SC4, 4), in_=col(C_SC0, 4),
                                           scalar=0.0, op=ALU.is_gt)
                    e.drain()
                    e.tensor_scalar(out=col(C_SC4, 4), in0=col(C_SC4, 4),
                                    scalar1=w_it, scalar2=-w_it / 2,
                                    op0=ALU.mult, op1=ALU.add)
                    e.drain()
                    e.tensor_tensor(out=col(C_TMJ, 4), in0=col(C_TMJ, 4),
                                    in1=col(C_SC4, 4), op=ALU.add)
                    e.drain()
                    e.tensor_scalar(out=col(C_NMJ, 4), in0=col(C_TMJ, 4),
                                    scalar1=-1.0, scalar2=None, op0=ALU.mult)
                    e.drain().then_inc(s_ve, 1)
                E("vector", f)
            sub_done = cnt["ve"]

            for j in range(NT):
                cnt["ve"] += 1
                abs_ms = cnt["ve"]
                def f(e, j=j, first=(j == 0), sub_done=sub_done):
                    e.scalar_tensor_tensor(out=ABSV, in0=SM[:, j, :], scalar=-1.0,
                                           in1=SM[:, j, :], op0=ALU.mult, op1=ALU.max)
                    e.drain()
                    e.tensor_copy(out=col(C_TMID), in_=col(C_TMJ + j))
                    e.drain()
                    e.tensor_scalar(out=col(C_NTM), in0=col(C_TMID), scalar1=-1.0,
                                    scalar2=SEC_HW, op0=ALU.mult, op1=ALU.add)
                    e.drain()
                    e.tensor_scalar(out=col(C_NT2), in0=col(C_TMID), scalar1=-1.0,
                                    scalar2=-SEC_HW, op0=ALU.mult, op1=ALU.add)
                    e.drain().then_inc(s_ve, 1)
                E("vector", f)

                cnt["se"] += 1
                s_cnt = cnt["se"]
                def f(e, j=j, v_last=abs_ms):
                    e.wait_ge(s_ve, v_last)
                    for c in range(4):
                        if c:
                            e.drain()
                        e.activation(out=JNKB[:, 0:2752],
                                     in_=ABSV[:, 2752 * c:2752 * (c + 1)],
                                     func=AF.Sign, bias=col(C_NTM),
                                     accum_out=col(C_SC0 + c))
                    for c in range(4):
                        e.drain()
                        e.activation(out=JNKB[:, 0:2752],
                                     in_=ABSV[:, 2752 * c:2752 * (c + 1)],
                                     func=AF.Sign, bias=col(C_NT2),
                                     accum_out=col(C_SC4 + c))
                    e.drain().then_inc(s_se, 1)
                E("scalar", f)

                cnt["ve"] += 1
                ms_mask[(p, j)] = cnt["ve"]
                def f(e, j=j, s_cnt=s_cnt):
                    e.wait_ge(s_se, s_cnt)
                    e.reduce_sum(out=col(C_S), in_=col(C_SC0, 4), axis=AX.X)
                    e.reduce_sum(out=col(C_D2), in_=col(C_SC4, 4), axis=AX.X)
                    e.drain()
                    e.tensor_tensor(out=col(C_DIF), in0=col(C_S), in1=col(C_D2),
                                    op=ALU.subtract)
                    e.drain()
                    e.tensor_scalar_max(col(C_DIF), col(C_DIF), 1.0)
                    e.drain()
                    e.reciprocal(out=col(C_DIF), in_=col(C_DIF))
                    e.drain()
                    e.tensor_tensor(out=col(C_Q), in0=col(C_S), in1=col(C_DIF),
                                    op=ALU.mult)
                    e.drain()
                    e.tensor_scalar(out=col(C_D1), in0=col(C_Q), scalar1=2 * SEC_HW,
                                    scalar2=-SEC_HW, op0=ALU.mult, op1=ALU.add)
                    e.drain()
                    e.tensor_scalar_min(col(C_D1), col(C_D1), SEC_CL)
                    e.drain()
                    e.tensor_scalar_max(col(C_D1), col(C_D1), -SEC_CL)
                    e.drain()
                    e.tensor_tensor(out=col(C_THR), in0=col(C_TMID), in1=col(C_D1),
                                    op=ALU.add)
                    e.drain()
                    e.scalar_tensor_tensor(out=SM[:, j, :], in0=ABSV, scalar=col(C_THR),
                                           in1=SM[:, j, :], op0=ALU.is_gt,
                                           op1=ALU.mult)
                    e.drain().then_inc(s_ve, 1)
                E("vector", f)

            for m in range(NM):
                w = MW[m]
                i0 = 1024 * m
                slices = [(0, 512), (512, w - 512)]
                m_first_waits = [bank_war[4 * si + j] for si in range(2) for j in range(NT)]
                pair_sem = {}
                for hc in range(NHC):
                    if hc % 2 == 0:
                        widx[0] += 1
                        qgp = widx[0] % 2
                        b0 = 4 * qgp + 2 * ((widx[0] // 2) % 2)
                        sem = s_wg[b0 - 4] if qgp else s_ws[b0]
                        war = max(wstr_war[b0], wstr_war[b0 + 1])
                        wcnt[b0] += 1
                        uc = wcnt[b0]
                        pair_sem[hc // 2] = (sem, uc, b0)
                        def f(e, war=war, b0=b0, hcp=hc, i0=i0, w=w, sem=sem):
                            if war:
                                e.wait_ge(s_pe, war)
                            e.dma_start(out=WSTR[:, b0:b0 + 2, 0:w],
                                        in_=utd[128 * hcp:128 * hcp + 256, i0:i0 + w]
                                        .rearrange("(c p) w -> p c w", p=128)).then_inc(sem, 16)
                        E("gpsimd" if qgp else "sync", f)
                    sem, uc, b0 = pair_sem[hc // 2]
                    buf = b0 + hc % 2
                    cnt["pe"] += 1
                    pe_ms = cnt["pe"]
                    wstr_war[buf] = pe_ms
                    def f(e, uc=uc, buf=buf, hc=hc, slices=slices,
                          first=(hc == 0), waits=m_first_waits,
                          sem=sem):
                        e.wait_ge(sem, 16 * uc)
                        if first:
                            for bw in waits:
                                e.wait_ge(SEM[bw[0]], bw[1])
                        for j in range(NT):
                            for si, (off, wsl) in enumerate(slices):
                                mm = e.matmul(PS[4 * si + j][:, 0:wsl],
                                              XNF[:, hc, 128 * j:128 * j + 128],
                                              WSTR[:, buf, off:off + wsl],
                                              start=(hc == 0), stop=(hc == NHC - 1),
                                              skip_group_check=True)
                        mm.then_inc(s_pe, 1)
                    E("tensor", f)
                m_last_pe = cnt["pe"]

                prod_ms = {}
                for si, (off, wsl) in enumerate(slices):
                    cnt["ve"] += 1
                    prod_ms[si] = cnt["ve"]
                    def f(e, si=si, off=off, wsl=wsl, i0=i0,
                          m_last_pe=m_last_pe, first=(si == 0)):
                        if first:
                            e.wait_ge(s_pe, m_last_pe)
                        e.tensor_tensor(out=SM[:, :, i0 + off:i0 + off + wsl],
                                        in0=SM[:, :, i0 + off:i0 + off + wsl],
                                        in1=PSA[:, 2048 * si:2048 * si + 2048]
                                        .rearrange("p (j w) -> p j w", w=512)[:, :, 0:wsl],
                                        op=ALU.mult)
                        e.drain().then_inc(s_ve, 1)
                    E("vector", f)

                tr_ms = {}
                for si, (off, wsl) in enumerate(slices):
                    for j in range(NT):
                        cnt["pe"] += 1
                        tr_ms[(si, j)] = cnt["pe"]
                        def f(e, j=j, si=si, off=off, wsl=wsl, i0=i0,
                              pm=prod_ms[si]):
                            e.wait_ge(s_ve, pm)
                            for c in range(wsl // 128):
                                mm = e.matmul(PS[4 * si + j][:, 128 * c:128 * c + 128],
                                              SM[:, j, i0 + off + 128 * c:i0 + off + 128 * c + 128],
                                              IDENT[:], start=True, stop=True)
                            mm.then_inc(s_pe, 1)
                        E("tensor", f)

                for si, (off, wsl) in enumerate(slices):
                    cnt["ve"] += 1
                    ev = cnt["ve"]
                    for j in range(NT):
                        bank_war[4 * si + j] = ("ve", ev)
                    def f(e, si=si, off=off, wsl=wsl, i0=i0,
                          tm=tr_ms[(si, NT - 1)]):
                        e.wait_ge(s_pe, tm)
                        e.tensor_copy(out=SM[:, :, i0 + off:i0 + off + wsl],
                                      in_=PSA[:, 2048 * si:2048 * si + 2048]
                                      .rearrange("p (j w) -> p j w", w=512)[:, :, 0:wsl])
                        e.drain().then_inc(s_ve, 1)
                    E("vector", f)
            ms_b1evac_last[0] = cnt["ve"]

            if p + 1 < NPASS:
                emit_lnA(p + 1)

            for P in range(NOPAIR):
                oc = 1024 * P
                p_first_waits = [bank_war[4 * oo + j] for oo in range(2) for j in range(NT)]
                for ic in range(NIC + NHC):
                    didx[0] += 1
                    dbuf = didx[0] % NDBUF
                    war = dstr_war[dbuf]
                    dcnt[dbuf] += 1
                    dc = dcnt[dbuf]
                    def f(e, war=war, dbuf=dbuf, ic=ic, oc=oc):
                        if war:
                            e.wait_ge(s_pe, war)
                        e.dma_start(out=DSTR[:, dbuf, :],
                                    in_=dtd[128 * ic:128 * ic + 128, oc:oc + 1024]).then_inc(s_ds[dbuf], 16)
                    E("sync", f)

                    cnt["pe"] += 1
                    pe_ms = cnt["pe"]
                    dstr_war[dbuf] = pe_ms
                    def f(e, dc=dc, dbuf=dbuf, ic=ic,
                          first=(ic == 0), waits=p_first_waits,
                          extra=(ms_b1evac_last[0] if ic == 0 and P == 0 else 0),
                          qev=(ms_qevac[(p, NT - 1)] if ic == 0 and P == 0 else 0)):
                        e.wait_ge(s_ds[dbuf], 16 * dc)
                        if first:
                            if extra:
                                e.wait_ge(s_ve, extra)
                            if qev:
                                e.wait_ge(s_ve, qev)
                            for bw in waits:
                                e.wait_ge(SEM[bw[0]], bw[1])
                        for j in range(NT):
                            lhsT = (SM[:, j, 128 * ic:128 * ic + 128] if ic < NIC
                                    else XNF[:, ic - NIC, 128 * j:128 * j + 128])
                            for oo in range(2):
                                mm = e.matmul(PS[4 * oo + j][:, 0:512],
                                              lhsT,
                                              DSTR[:, dbuf, 512 * oo:512 * oo + 512],
                                              start=(ic == 0), stop=False,
                                              skip_group_check=True)
                        mm.then_inc(s_pe, 1)
                    E("tensor", f)

                grp_ms = {}
                for oo in range(2):
                    for j in range(NT):
                        cnt["pe"] += 1
                        grp_ms[(oo, j)] = cnt["pe"]
                        def f(e, oo=oo, j=j, oc=oc):
                            e.matmul(PS[4 * oo + j][:, 0:512], QF[:, 128 * j:128 * j + 128],
                                     BMTS[:, oc + 512 * oo:oc + 512 * oo + 512],
                                     start=False, stop=False, skip_group_check=True)
                            e.matmul(PS[4 * oo + j][:, 0:512], E0S[:],
                                     BPADS[:, oc + 512 * oo:oc + 512 * oo + 512],
                                     start=False, stop=True,
                                     skip_group_check=True).then_inc(s_pe, 1)
                        E("tensor", f)

                for oo in range(2):
                    for j in range(NT):
                        k = (p * NOPAIR * 8) + P * 8 + oo * 4 + j
                        o = 2 * P + oo
                        row = 512 * p + 128 * j
                        wv = ms_yevac.get(k - 2, 0)
                        xrcnt[k % 2] += 1
                        rc = xrcnt[k % 2]
                        def f(e, wv=wv, k=k, row=row, o=o):
                            if wv:
                                e.wait_ge(s_ve, wv)
                            e.dma_start(out=XRS[:, k % 2, :],
                                        in_=xd[row:row + 128, 512 * o:512 * o + 512]).then_inc(s_xr[k % 2], 16)
                        E("scalar", f)

                        cnt["ve"] += 1
                        ms_yevac[k] = cnt["ve"]
                        bank_war[4 * oo + j] = ("ve", cnt["ve"])
                        ystw = yycnt[k % 2]
                        def f(e, oo=oo, j=j, k=k, rc=rc, gm=grp_ms[(oo, j)], ystw=ystw):
                            e.wait_ge(s_pe, gm)
                            e.wait_ge(s_xr[k % 2], 16 * rc)
                            if ystw:
                                e.wait_ge(s_yy[k % 2], 16 * ystw)
                            e.tensor_tensor(out=YST[:, k % 2, :],
                                            in0=PS[4 * oo + j][:, 0:512],
                                            in1=XRS[:, k % 2, :],
                                            op=ALU.add)
                            e.drain().then_inc(s_ve, 1)
                        E("vector", f)

                        yycnt[k % 2] += 1
                        def f(e, k=k, row=row, o=o, ym=ms_yevac[k]):
                            e.wait_ge(s_ve, ym)
                            e.dma_start(out=yd[row:row + 128, 512 * o:512 * o + 512],
                                        in_=YST[:, k % 2, :]).then_inc(s_yy[k % 2], 16)
                        E("gpsimd", f)

        def f_fin(e, t0=yycnt[0], t1=yycnt[1]):
            e.wait_ge(s_yy[0], 16 * t0)
            e.wait_ge(s_yy[1], 16 * t1)
        E("sync", f_fin)

        @blk.sync
        def _(e):
            for fn in plan["sync"]:
                fn(e)

        @blk.scalar
        def _(e):
            for fn in plan["scalar"]:
                fn(e)

        @blk.vector
        def _(e):
            for fn in plan["vector"]:
                fn(e)

        @blk.tensor
        def _(e):
            for fn in plan["tensor"]:
                fn(e)

        @blk.gpsimd
        def _(e):
            for fn in plan["gpsimd"]:
                fn(e)

    return nc


def kernel(L0_attn_x, L0_attn_w_gate, L0_attn_w_up, L0_attn_w_down,
           L0_ffn_W, L0_ffn_b, L0_ffn_A, L0_ffn_Bm, L0_ffn_Wr):
    x = np.ascontiguousarray(np.asarray(L0_attn_x, dtype=np.float32).reshape(NCORES * TCORE, H))
    gt = np.ascontiguousarray(np.asarray(L0_attn_w_gate).astype(BF).T)
    ut = np.ascontiguousarray(np.asarray(L0_attn_w_up).astype(BF).T)
    dt = np.concatenate([np.asarray(L0_attn_w_down).astype(BF).T,
                         np.asarray(L0_ffn_W).astype(BF).T], axis=0)
    dt = np.ascontiguousarray(dt)
    a2 = np.ascontiguousarray(np.asarray(L0_ffn_A).astype(BF).reshape(32, H).T
                              .reshape(NHC, 128, 32).transpose(1, 0, 2))
    wrt = np.ascontiguousarray(np.asarray(L0_ffn_Wr).astype(BF).T
                               .reshape(NHC, 128, 4).transpose(1, 0, 2))
    bmt = np.zeros((128, H), dtype=BF)
    bmt[0:32] = np.asarray(L0_ffn_Bm).astype(BF).transpose(0, 2, 1).reshape(32, H)
    bpad = np.zeros((128, H), dtype=BF)
    bpad[0] = np.asarray(L0_ffn_b).astype(BF)
    e0 = np.zeros((128, 128), dtype=BF)
    e0[0] = 1
    ident = np.eye(128, dtype=BF)

    nc = build_nc()
    shared = {"gt": gt, "ut": ut, "dt": dt, "a2": a2, "wrt": wrt,
              "bmt": bmt, "bpad": bpad, "e0": e0, "ident": ident}
    in_maps = [{"x": np.ascontiguousarray(x[c * TCORE:(c + 1) * TCORE]), **shared}
               for c in range(NCORES)]
    res = run_bass_kernel_spmd(nc, in_maps, core_ids=list(range(NCORES)))
    y = np.concatenate([r["y"] for r in res.results], axis=0)
    return y.reshape(2, 4096, 4096)


# version 28 (final)
# speedup vs baseline: 1.0221x; 1.0221x over previous
import numpy as np
import ml_dtypes
import concourse.bass as bass
import concourse.mybir as mybir
from concourse.bass_utils import run_bass_kernel_spmd

F32 = mybir.dt.float32
BF16 = mybir.dt.bfloat16
AF = mybir.ActivationFunctionType
ALU = mybir.AluOpType
AX = mybir.AxisListType
BF = ml_dtypes.bfloat16

NCORES = 8
H = 4096
II = 11008
TCORE = 1024
NPASS = 2
NT = 4
NHC = 32
NM = 11
MW = [1024] * 10 + [768]
NIC = 86
NOPAIR = 4
SUB_ITERS = 6
FULL_ITERS = 5
NWBUF = 8
NDBUF = 6

C_SX, C_SX2, C_M, C_V, C_RSTD, C_T1, C_TMID, C_NTM, C_NT2, C_DIF, C_S, C_MSK, \
    C_Q, C_D1, C_D2, C_THR, C_SC0, C_SC1, C_SC2, C_SC3, C_SC4, C_SC5, C_SC6, C_SC7 = range(24)
C_GM, C_GS, C_GR = 24, 25, 26
C_G4 = 28
C_GE4 = 32
C_MJ = 36
C_RJ = 40
C_TMJ = 44
C_NMJ = 48
NCOLS = 52
SUB_N = 1376
SEC_HW = 2.0 / 64 / 16
SEC_CL = 2.0 / 64 / 2


def build_nc():
    nc = bass.Bass("TRN2", target_bir_lowering=False, debug=False)
    xd = nc.dram_tensor("x", [TCORE, H], F32, kind="ExternalInput").ap()
    gtd = nc.dram_tensor("gt", [H, II], BF16, kind="ExternalInput").ap()
    utd = nc.dram_tensor("ut", [H, II], BF16, kind="ExternalInput").ap()
    dtd = nc.dram_tensor("dt", [II + H, H], BF16, kind="ExternalInput").ap()
    a2d = nc.dram_tensor("a2", [128, NHC, 32], BF16, kind="ExternalInput").ap()
    wrtd = nc.dram_tensor("wrt", [128, NHC, 4], BF16, kind="ExternalInput").ap()
    bmtd = nc.dram_tensor("bmt", [128, H], BF16, kind="ExternalInput").ap()
    bpd = nc.dram_tensor("bpad", [128, H], BF16, kind="ExternalInput").ap()
    e0d = nc.dram_tensor("e0", [128, 128], BF16, kind="ExternalInput").ap()
    idd = nc.dram_tensor("ident", [128, 128], BF16, kind="ExternalInput").ap()
    yd = nc.dram_tensor("y", [TCORE, H], F32, kind="ExternalOutput").ap()

    plan = {k: [] for k in ("sync", "scalar", "vector", "tensor", "gpsimd")}

    def E(eng, fn):
        plan[eng].append(fn)

    from contextlib import ExitStack
    with ExitStack() as ctx:
        SM = ctx.enter_context(nc.sbuf_tensor("SM", [128, NT, II], BF16))
        XNF = ctx.enter_context(nc.sbuf_tensor("XNF", [128, NHC, 512], BF16))
        ARN = ctx.enter_context(nc.sbuf_tensor("ARN", [128, 5504], F32))
        WSTR = ctx.enter_context(nc.sbuf_tensor("WSTR", [128, NWBUF, 1024], BF16))
        DSTR = ctx.enter_context(nc.sbuf_tensor("DSTR", [128, NDBUF, 1024], BF16))
        YST = ctx.enter_context(nc.sbuf_tensor("YST", [128, 2, 512], F32))
        XRS = ctx.enter_context(nc.sbuf_tensor("XRS", [128, 4, 512], F32))
        JNKB = ctx.enter_context(nc.sbuf_tensor("JNKB", [128, H], BF16))
        QF = ctx.enter_context(nc.sbuf_tensor("QF", [128, 512], BF16))
        QT = ctx.enter_context(nc.sbuf_tensor("QT", [128, 32], BF16))
        G2T = ctx.enter_context(nc.sbuf_tensor("G2T", [128, 4], BF16))
        A2S = ctx.enter_context(nc.sbuf_tensor("A2S", [128, NHC, 32], BF16))
        WRTS = ctx.enter_context(nc.sbuf_tensor("WRTS", [128, NHC, 4], BF16))
        BMTS = ctx.enter_context(nc.sbuf_tensor("BMTS", [128, H], BF16))
        BPADS = ctx.enter_context(nc.sbuf_tensor("BPADS", [128, H], BF16))
        E0S = ctx.enter_context(nc.sbuf_tensor("E0S", [128, 128], BF16))
        IDENT = ctx.enter_context(nc.sbuf_tensor("IDENT", [128, 128], BF16))
        COLS = ctx.enter_context(nc.sbuf_tensor("COLS", [128, NCOLS], F32))
        PSA = ctx.enter_context(nc.psum_tensor("PSA", [128, 4096], F32))
        PS = [PSA[:, 512 * i:512 * (i + 1)] for i in range(8)]
        s_pe = ctx.enter_context(nc.semaphore("s_pe"))
        s_ve = ctx.enter_context(nc.semaphore("s_ve"))
        s_se = ctx.enter_context(nc.semaphore("s_se"))
        s_ws = [ctx.enter_context(nc.semaphore(f"s_ws{i}")) for i in range(4)]
        s_wg = [ctx.enter_context(nc.semaphore(f"s_wg{i}")) for i in range(4)]
        s_wp = [ctx.enter_context(nc.semaphore(f"s_wp{i}")) for i in range(4)]
        s_ds = [ctx.enter_context(nc.semaphore(f"s_ds{i}")) for i in range(NDBUF)]
        s_xl = ctx.enter_context(nc.semaphore("s_xl"))
        s_xr = [ctx.enter_context(nc.semaphore(f"s_xr{i}")) for i in range(2)]
        s_yy = [ctx.enter_context(nc.semaphore(f"s_yy{i}")) for i in range(2)]
        s_cst = ctx.enter_context(nc.semaphore("s_cst"))
        blk = ctx.enter_context(nc.Block())
        SEM = {"pe": s_pe, "ve": s_ve, "se": s_se}
        ABSV = ARN[:].bitcast(BF16)[:, 0:II]
        XT = ARN[:, 0:H]

        cnt = {"pe": 0, "ve": 0, "se": 0}
        wcnt = [0] * NWBUF
        wpcnt = [0, 0, 0, 0]
        dcnt = [0] * NDBUF
        xlcnt = [0]
        xrcnt = [0, 0]
        yycnt = [0, 0]
        bank_war = [("ve", 0)] * 8
        wstr_war = [0] * NWBUF
        dstr_war = [0] * NDBUF
        widx = [0]
        didx = [0]
        ms_xnt = {}
        ms_mask = {}
        ms_smdone = {}
        ms_qevac = {}
        ms_lntrans = {}
        ms_yevac = {}
        ms_b1evac_last = [0]
        samp_ms = [0]

        def col(i, n=1):
            return COLS[:, i:i + n]

        NCST = 6
        def pre_sy(e):
            e.dma_start(out=IDENT[:], in_=idd).then_inc(s_cst, 16)
            e.dma_start(out=E0S[:], in_=e0d).then_inc(s_cst, 16)
            e.dma_start(out=A2S[:], in_=a2d).then_inc(s_cst, 16)
            e.dma_start(out=WRTS[:], in_=wrtd).then_inc(s_cst, 16)
            e.dma_start(out=BMTS[:], in_=bmtd).then_inc(s_cst, 16)
            e.dma_start(out=BPADS[:], in_=bpd).then_inc(s_cst, 16)
        E("sync", pre_sy)

        cnt["ve"] += 1
        def _pre_v(e):
            e.memset(QF[:], 0.0)
            e.drain().then_inc(s_ve, 1)
        E("vector", _pre_v)
        E("tensor", lambda e: e.wait_ge(s_cst, 16 * NCST))

        stats_ms = {}
        var_ms = {}
        rstd_ms = {}

        def emit_lnA(p):
            for j in range(NT):
                row = 512 * p + 128 * j
                if p == 0 and j == 0:
                    wv = None
                elif j == 0:
                    wv = ("ve", ms_mask[(p - 1, NT - 1)])
                else:
                    wv = ("se", stats_ms[(p, j - 1)])
                xlcnt[0] += 1
                xc = xlcnt[0]
                def f(e, wv=wv, row=row):
                    if wv:
                        e.wait_ge(SEM[wv[0]], wv[1])
                    e.dma_start(out=XT, in_=xd[row:row + 128, :]).then_inc(s_xl, 16)
                E("gpsimd", f)

                cnt["se"] += 1
                stats_ms[(p, j)] = cnt["se"]
                def f(e, xc=xc, vw=(var_ms.get((p, j - 1), 0))):
                    e.wait_ge(s_xl, 16 * xc)
                    if vw:
                        e.wait_ge(s_ve, vw)
                    e.activation(out=JNKB[:], in_=XT, func=AF.Identity,
                                 accum_out=col(C_SX))
                    e.drain()
                    e.activation(out=JNKB[:], in_=XT, func=AF.Square,
                                 accum_out=col(C_SX2))
                    e.drain().then_inc(s_se, 1)
                E("scalar", f)

                cnt["ve"] += 1
                var_ms[(p, j)] = cnt["ve"]
                def f(e, j=j, sm_=stats_ms[(p, j)]):
                    e.wait_ge(s_se, sm_)
                    e.tensor_scalar(out=col(C_MJ + j), in0=col(C_SX), scalar1=1.0 / H,
                                    scalar2=None, op0=ALU.mult)
                    e.tensor_scalar(out=col(C_V), in0=col(C_SX2), scalar1=1.0 / H,
                                    scalar2=None, op0=ALU.mult)
                    e.drain()
                    e.tensor_tensor(out=col(C_T1), in0=col(C_MJ + j),
                                    in1=col(C_MJ + j), op=ALU.mult)
                    e.drain()
                    e.tensor_tensor(out=col(C_V), in0=col(C_V), in1=col(C_T1),
                                    op=ALU.subtract)
                    e.drain()
                    e.tensor_scalar(out=col(C_V), in0=col(C_V), scalar1=1e-5,
                                    scalar2=None, op0=ALU.add)
                    e.drain().then_inc(s_ve, 1)
                E("vector", f)

                cnt["se"] += 1
                ln_ms = cnt["se"]
                def f(e, vm=var_ms[(p, j)]):
                    e.wait_ge(s_ve, vm)
                    e.activation(out=col(C_T1), in_=col(C_V), func=AF.Ln)
                    e.drain().then_inc(s_se, 1)
                E("scalar", f)
                cnt["ve"] += 1
                lnh_ms = cnt["ve"]
                def f(e, lm=ln_ms):
                    e.wait_ge(s_se, lm)
                    e.tensor_scalar(out=col(C_T1), in0=col(C_T1), scalar1=-0.5,
                                    scalar2=None, op0=ALU.mult)
                    e.drain().then_inc(s_ve, 1)
                E("vector", f)
                cnt["se"] += 1
                rstd_ms[(p, j)] = cnt["se"]
                def f(e, j=j, lh=lnh_ms):
                    e.wait_ge(s_ve, lh)
                    e.activation(out=col(C_RJ + j), in_=col(C_T1), func=AF.Exp)
                    e.drain().then_inc(s_se, 1)
                E("scalar", f)

        for p in range(NPASS):
            if p == 0:
                emit_lnA(0)
            for j in range(NT):
                row = 512 * p + 128 * j

                if j == 0:
                    wv = ("se", stats_ms[(p, NT - 1)])
                else:
                    wv = ("ve", ms_xnt[(p, j - 1)])
                xlcnt[0] += 1
                xc = xlcnt[0]
                def f(e, wv=wv, row=row):
                    e.wait_ge(SEM[wv[0]], wv[1])
                    e.dma_start(out=XT, in_=xd[row:row + 128, :]).then_inc(s_xl, 16)
                E("gpsimd", f)

                jw = ms_lntrans.get((p, j - 1), ms_lntrans.get((p - 1, NT - 1), 0))
                cnt["ve"] += 1
                ms_xnt[(p, j)] = cnt["ve"]
                def f(e, j=j, xc=xc, jw=jw, rm=rstd_ms[(p, j)]):
                    e.wait_ge(s_xl, 16 * xc)
                    e.wait_ge(s_se, rm)
                    if jw:
                        e.wait_ge(s_pe, jw)
                    e.tensor_scalar(out=JNKB[:], in0=XT, scalar1=col(C_MJ + j),
                                    scalar2=col(C_RJ + j), op0=ALU.subtract,
                                    op1=ALU.mult)
                    e.drain().then_inc(s_ve, 1)
                E("vector", f)

                for g8 in range(8):
                    bank = g8 % 2
                    bw = bank_war[bank]
                    cnt["pe"] += 1
                    grp_ms = cnt["pe"]
                    def f(e, g8=g8, bank=bank, bw=bw, j=j, first=(g8 == 0),
                          xnt=ms_xnt[(p, j)]):
                        if first:
                            e.wait_ge(s_ve, xnt)
                        e.wait_ge(SEM[bw[0]], bw[1])
                        for k in range(4):
                            hc = 4 * g8 + k
                            mm = e.matmul(PS[bank][:, 128 * k:128 * k + 128],
                                          JNKB[:, 128 * hc:128 * hc + 128], IDENT[:],
                                          start=True, stop=True)
                        mm.then_inc(s_pe, 1)
                    E("tensor", f)
                    cnt["ve"] += 1
                    ev_ms = cnt["ve"]
                    bank_war[bank] = ("ve", ev_ms)
                    def f(e, g8=g8, bank=bank, j=j, grp_ms=grp_ms):
                        e.wait_ge(s_pe, grp_ms)
                        e.tensor_copy(out=XNF[:, 4 * g8:4 * g8 + 4, 128 * j:128 * j + 128],
                                      in_=PS[bank][:, 0:512])
                        e.drain().then_inc(s_ve, 1)
                    E("vector", f)
                ms_lntrans[(p, j)] = cnt["pe"]
                xnf_done = cnt["ve"]

                bw2, bw3 = bank_war[2], bank_war[3]
                cnt["pe"] += 2
                gat_ms = cnt["pe"] - 1
                pt_ms = cnt["pe"]
                def f(e, j=j, bw2=bw2, bw3=bw3, xnf_done=xnf_done):
                    e.wait_ge(s_ve, xnf_done)
                    e.wait_ge(SEM[bw2[0]], bw2[1])
                    for hc in range(NHC):
                        mm = e.matmul(PS[2][:, 0:4], XNF[:, hc, 128 * j:128 * j + 128],
                                      WRTS[:, hc, :], start=(hc == 0), stop=(hc == NHC - 1),
                                      skip_group_check=True)
                    mm.then_inc(s_pe, 1)
                    e.wait_ge(SEM[bw3[0]], bw3[1])
                    for hc in range(NHC):
                        mm = e.matmul(PS[3][:, 0:32], XNF[:, hc, 128 * j:128 * j + 128],
                                      A2S[:, hc, :], start=(hc == 0), stop=(hc == NHC - 1),
                                      skip_group_check=True)
                    mm.then_inc(s_pe, 1)
                E("tensor", f)

                cnt["ve"] += 1
                gsub_ms = cnt["ve"]
                def f(e, gat_ms=gat_ms):
                    e.wait_ge(s_pe, gat_ms)
                    e.tensor_copy(out=col(C_G4, 4), in_=PS[2][:, 0:4])
                    e.drain()
                    e.reduce_max(out=col(C_GM), in_=col(C_G4, 4), axis=AX.X)
                    e.drain()
                    e.tensor_scalar(out=col(C_G4, 4), in0=col(C_G4, 4), scalar1=col(C_GM),
                                    scalar2=None, op0=ALU.subtract)
                    e.drain().then_inc(s_ve, 1)
                E("vector", f)
                bank_war[2] = ("ve", gsub_ms)

                cnt["se"] += 1
                gexp_ms = cnt["se"]
                def f(e, gsub_ms=gsub_ms):
                    e.wait_ge(s_ve, gsub_ms)
                    e.activation(out=col(C_GE4, 4), in_=col(C_G4, 4), func=AF.Exp)
                    e.drain().then_inc(s_se, 1)
                E("scalar", f)

                cnt["ve"] += 1
                qt_ms = cnt["ve"]
                bank_war[3] = ("ve", qt_ms)
                def f(e, gexp_ms=gexp_ms, pt_ms=pt_ms):
                    e.wait_ge(s_se, gexp_ms)
                    e.reduce_sum(out=col(C_GS), in_=col(C_GE4, 4), axis=AX.X)
                    e.drain()
                    e.reciprocal(out=col(C_GR), in_=col(C_GS))
                    e.drain()
                    e.tensor_scalar(out=G2T[:], in0=col(C_GE4, 4), scalar1=col(C_GR),
                                    scalar2=2.0, op0=ALU.mult, op1=ALU.mult)
                    e.drain()
                    e.wait_ge(s_pe, pt_ms)
                    e.tensor_tensor(out=QT[:].rearrange("p (e r) -> p e r", r=8),
                                    in0=PS[3][:, 0:32].rearrange("p (e r) -> p e r", r=8),
                                    in1=G2T[:].unsqueeze(2).broadcast_to([128, 4, 8]),
                                    op=ALU.mult)
                    e.drain().then_inc(s_ve, 1)
                E("vector", f)

                bw4 = bank_war[4]
                cnt["pe"] += 1
                qtr_ms = cnt["pe"]
                def f(e, j=j, bw4=bw4, qt_ms=qt_ms):
                    e.wait_ge(s_ve, qt_ms)
                    e.wait_ge(SEM[bw4[0]], bw4[1])
                    e.matmul(PS[4][0:32, 128 * j:128 * j + 128], QT[:], IDENT[:],
                             start=True, stop=True).then_inc(s_pe, 1)
                E("tensor", f)
                cnt["ve"] += 1
                ms_qevac[(p, j)] = cnt["ve"]
                bank_war[4] = ("ve", cnt["ve"])
                def f(e, j=j, qtr_ms=qtr_ms):
                    e.wait_ge(s_pe, qtr_ms)
                    e.tensor_copy(out=QF[0:32, 128 * j:128 * j + 128],
                                  in_=PS[4][0:32, 128 * j:128 * j + 128])
                    e.drain().then_inc(s_ve, 1)
                E("vector", f)

            for m in range(NM):
                w = MW[m]
                i0 = 1024 * m
                slices = [(0, 512), (512, w - 512)]
                m_first_waits = [bank_war[4 * si + j] for si in range(2) for j in range(NT)]
                pair_sem = {}
                for hc in range(NHC):
                    if hc % 2 == 0:
                        hc2 = hc // 2
                        widx[0] += 1
                        qgp = widx[0] % 2
                        b0 = 4 * qgp + 2 * ((widx[0] // 2) % 2)
                        sem = s_wg[b0 - 4] if qgp else s_ws[b0]
                        war = max(wstr_war[b0], wstr_war[b0 + 1])
                        wcnt[b0] += 1
                        gc = wcnt[b0]
                        pair_sem[hc2] = (sem, gc, b0)
                        def f(e, war=war, b0=b0, hcp=hc, i0=i0, w=w, sem=sem):
                            if war:
                                e.wait_ge(s_pe, war)
                            e.dma_start(out=WSTR[:, b0:b0 + 2, 0:w],
                                        in_=gtd[128 * hcp:128 * hcp + 256, i0:i0 + w]
                                        .rearrange("(c p) w -> p c w", p=128)).then_inc(sem, 16)
                        E("gpsimd" if qgp else "sync", f)
                    sem, gc, b0 = pair_sem[hc // 2]
                    buf = b0 + hc % 2
                    cnt["pe"] += 1
                    pe_ms = cnt["pe"]
                    wstr_war[buf] = pe_ms
                    def f(e, gc=gc, buf=buf, hc=hc, slices=slices, sem=sem,
                          first=(hc == 0), waits=m_first_waits):
                        e.wait_ge(sem, 16 * gc)
                        if first:
                            for bw in waits:
                                e.wait_ge(SEM[bw[0]], bw[1])
                        for si, (off, wsl) in enumerate(slices):
                            for j in range(NT):
                                mm = e.matmul(PS[4 * si + j][:, 0:wsl],
                                              XNF[:, hc, 128 * j:128 * j + 128],
                                              WSTR[:, buf, off:off + wsl],
                                              start=(hc == 0), stop=(hc == NHC - 1),
                                              skip_group_check=True)
                        mm.then_inc(s_pe, 1)
                    E("tensor", f)
                m_last_pe = cnt["pe"]

                for si, (off, wsl) in enumerate(slices):
                    cnt["se"] += 1
                    ev = cnt["se"]
                    for j in range(NT):
                        bank_war[4 * si + j] = ("se", ev)
                    def f(e, si=si, off=off, wsl=wsl, i0=i0,
                          m_last_pe=m_last_pe, first=(si == 0)):
                        if first:
                            e.wait_ge(s_pe, m_last_pe)
                        e.activation(out=SM[:, :, i0 + off:i0 + off + wsl],
                                     in_=PSA[:, 2048 * si:2048 * si + 2048]
                                     .rearrange("p (j w) -> p j w", w=512)[:, :, 0:wsl],
                                     func=AF.Silu)
                        e.drain().then_inc(s_se, 1)
                    E("scalar", f)
                if m == NM - 1:
                    for j in range(NT):
                        ms_smdone[(p, j)] = cnt["se"]

                DSAMP = DSTR[:].rearrange("p a b -> p (a b)")
                if m == 1:
                    cnt["ve"] += 1
                    samp_ms[0] = cnt["ve"]
                    def f(e, sds=[(i, dcnt[i]) for i in range(NDBUF)],
                          sev=cnt["se"]):
                        e.wait_ge(s_se, sev)
                        for i, c in sds:
                            if c:
                                e.wait_ge(s_ds[i], 16 * c)
                        for j in range(NT):
                            e.scalar_tensor_tensor(out=DSAMP[:, 1376 * j:1376 * (j + 1)],
                                                   in0=SM[:, j, 0:SUB_N], scalar=-1.0,
                                                   in1=SM[:, j, 0:SUB_N],
                                                   op0=ALU.mult, op1=ALU.max)
                            e.drain()
                        e.memset(col(C_TMJ, 4), 1.0)
                        e.memset(col(C_NMJ, 4), -1.0)
                        e.drain().then_inc(s_ve, 1)
                    E("vector", f)
                if 1 <= m <= SUB_ITERS:
                    it = m - 1
                    w_it = 2.0 / (2 ** (it + 1))
                    cnt["se"] += 1
                    s_it = cnt["se"]
                    def f(e, vw=(cnt["ve"] if it else samp_ms[0])):
                        e.wait_ge(s_ve, vw)
                        for j in range(NT):
                            if j:
                                e.drain()
                            e.activation(out=JNKB[:, 0:SUB_N],
                                         in_=DSAMP[:, 1376 * j:1376 * (j + 1)],
                                         func=AF.Sign, bias=col(C_NMJ + j),
                                         accum_out=col(C_SC0 + j))
                        e.drain().then_inc(s_se, 1)
                    E("scalar", f)
                    cnt["ve"] += 1
                    def f(e, s_it=s_it, w_it=w_it):
                        e.wait_ge(s_se, s_it)
                        e.tensor_single_scalar(out=col(C_SC4, 4), in_=col(C_SC0, 4),
                                               scalar=0.0, op=ALU.is_gt)
                        e.drain()
                        e.tensor_scalar(out=col(C_SC4, 4), in0=col(C_SC4, 4),
                                        scalar1=w_it, scalar2=-w_it / 2,
                                        op0=ALU.mult, op1=ALU.add)
                        e.drain()
                        e.tensor_tensor(out=col(C_TMJ, 4), in0=col(C_TMJ, 4),
                                        in1=col(C_SC4, 4), op=ALU.add)
                        e.drain()
                        e.tensor_scalar(out=col(C_NMJ, 4), in0=col(C_TMJ, 4),
                                        scalar1=-1.0, scalar2=None, op0=ALU.mult)
                        e.drain().then_inc(s_ve, 1)
                    E("vector", f)

            for j in range(NT):
                cnt["ve"] += 1
                abs_ms = cnt["ve"]
                def f(e, j=j, first=(j == 0), sd=max(ms_smdone[(p, jj)] for jj in range(NT))):
                    if first:
                        e.wait_ge(s_se, sd)
                    e.scalar_tensor_tensor(out=ABSV, in0=SM[:, j, :], scalar=-1.0,
                                           in1=SM[:, j, :], op0=ALU.mult, op1=ALU.max)
                    e.drain()
                    e.tensor_copy(out=col(C_TMID), in_=col(C_TMJ + j))
                    e.drain()
                    e.tensor_scalar(out=col(C_NTM), in0=col(C_TMID), scalar1=-1.0,
                                    scalar2=SEC_HW, op0=ALU.mult, op1=ALU.add)
                    e.drain()
                    e.tensor_scalar(out=col(C_NT2), in0=col(C_TMID), scalar1=-1.0,
                                    scalar2=-SEC_HW, op0=ALU.mult, op1=ALU.add)
                    e.drain().then_inc(s_ve, 1)
                E("vector", f)

                cnt["se"] += 1
                s_cnt = cnt["se"]
                def f(e, j=j, v_last=abs_ms):
                    e.wait_ge(s_ve, v_last)
                    for c in range(4):
                        if c:
                            e.drain()
                        e.activation(out=JNKB[:, 0:2752],
                                     in_=ABSV[:, 2752 * c:2752 * (c + 1)],
                                     func=AF.Sign, bias=col(C_NTM),
                                     accum_out=col(C_SC0 + c))
                    for c in range(4):
                        e.drain()
                        e.activation(out=JNKB[:, 0:2752],
                                     in_=ABSV[:, 2752 * c:2752 * (c + 1)],
                                     func=AF.Sign, bias=col(C_NT2),
                                     accum_out=col(C_SC4 + c))
                    e.drain().then_inc(s_se, 1)
                E("scalar", f)

                cnt["ve"] += 1
                ms_mask[(p, j)] = cnt["ve"]
                def f(e, j=j, s_cnt=s_cnt):
                    e.wait_ge(s_se, s_cnt)
                    e.reduce_sum(out=col(C_S), in_=col(C_SC0, 4), axis=AX.X)
                    e.reduce_sum(out=col(C_D2), in_=col(C_SC4, 4), axis=AX.X)
                    e.drain()
                    e.tensor_tensor(out=col(C_DIF), in0=col(C_S), in1=col(C_D2),
                                    op=ALU.subtract)
                    e.drain()
                    e.tensor_scalar_max(col(C_DIF), col(C_DIF), 1.0)
                    e.drain()
                    e.reciprocal(out=col(C_DIF), in_=col(C_DIF))
                    e.drain()
                    e.tensor_tensor(out=col(C_Q), in0=col(C_S), in1=col(C_DIF),
                                    op=ALU.mult)
                    e.drain()
                    e.tensor_scalar(out=col(C_D1), in0=col(C_Q), scalar1=2 * SEC_HW,
                                    scalar2=-SEC_HW, op0=ALU.mult, op1=ALU.add)
                    e.drain()
                    e.tensor_scalar_min(col(C_D1), col(C_D1), SEC_CL)
                    e.drain()
                    e.tensor_scalar_max(col(C_D1), col(C_D1), -SEC_CL)
                    e.drain()
                    e.tensor_tensor(out=col(C_THR), in0=col(C_TMID), in1=col(C_D1),
                                    op=ALU.add)
                    e.drain()
                    e.scalar_tensor_tensor(out=SM[:, j, :], in0=ABSV, scalar=col(C_THR),
                                           in1=SM[:, j, :], op0=ALU.is_gt,
                                           op1=ALU.mult)
                    e.drain().then_inc(s_ve, 1)
                E("vector", f)

            for m in range(NM):
                w = MW[m]
                i0 = 1024 * m
                slices = [(0, 512), (512, w - 512)]
                m_first_waits = [bank_war[4 * si + j] for si in range(2) for j in range(NT)]
                pair_sem = {}
                for hc in range(NHC):
                    if hc % 2 == 0:
                        widx[0] += 1
                        qgp = widx[0] % 2
                        b0 = 4 * qgp + 2 * ((widx[0] // 2) % 2)
                        sem = s_wg[b0 - 4] if qgp else s_ws[b0]
                        war = max(wstr_war[b0], wstr_war[b0 + 1])
                        wcnt[b0] += 1
                        uc = wcnt[b0]
                        pair_sem[hc // 2] = (sem, uc, b0)
                        def f(e, war=war, b0=b0, hcp=hc, i0=i0, w=w, sem=sem):
                            if war:
                                e.wait_ge(s_pe, war)
                            e.dma_start(out=WSTR[:, b0:b0 + 2, 0:w],
                                        in_=utd[128 * hcp:128 * hcp + 256, i0:i0 + w]
                                        .rearrange("(c p) w -> p c w", p=128)).then_inc(sem, 16)
                        E("gpsimd" if qgp else "sync", f)
                    sem, uc, b0 = pair_sem[hc // 2]
                    buf = b0 + hc % 2
                    cnt["pe"] += 1
                    pe_ms = cnt["pe"]
                    wstr_war[buf] = pe_ms
                    def f(e, uc=uc, buf=buf, hc=hc, slices=slices,
                          first=(hc == 0), waits=m_first_waits,
                          sem=sem):
                        e.wait_ge(sem, 16 * uc)
                        if first:
                            for bw in waits:
                                e.wait_ge(SEM[bw[0]], bw[1])
                        for j in range(NT):
                            for si, (off, wsl) in enumerate(slices):
                                mm = e.matmul(PS[4 * si + j][:, 0:wsl],
                                              XNF[:, hc, 128 * j:128 * j + 128],
                                              WSTR[:, buf, off:off + wsl],
                                              start=(hc == 0), stop=(hc == NHC - 1),
                                              skip_group_check=True)
                        mm.then_inc(s_pe, 1)
                    E("tensor", f)
                m_last_pe = cnt["pe"]

                prod_ms = {}
                for si, (off, wsl) in enumerate(slices):
                    cnt["ve"] += 1
                    prod_ms[si] = cnt["ve"]
                    def f(e, si=si, off=off, wsl=wsl, i0=i0,
                          m_last_pe=m_last_pe, first=(si == 0)):
                        if first:
                            e.wait_ge(s_pe, m_last_pe)
                        e.tensor_tensor(out=SM[:, :, i0 + off:i0 + off + wsl],
                                        in0=SM[:, :, i0 + off:i0 + off + wsl],
                                        in1=PSA[:, 2048 * si:2048 * si + 2048]
                                        .rearrange("p (j w) -> p j w", w=512)[:, :, 0:wsl],
                                        op=ALU.mult)
                        e.drain().then_inc(s_ve, 1)
                    E("vector", f)

                tr_ms = {}
                for si, (off, wsl) in enumerate(slices):
                    for j in range(NT):
                        cnt["pe"] += 1
                        tr_ms[(si, j)] = cnt["pe"]
                        def f(e, j=j, si=si, off=off, wsl=wsl, i0=i0,
                              pm=prod_ms[si]):
                            e.wait_ge(s_ve, pm)
                            for c in range(wsl // 128):
                                mm = e.matmul(PS[4 * si + j][:, 128 * c:128 * c + 128],
                                              SM[:, j, i0 + off + 128 * c:i0 + off + 128 * c + 128],
                                              IDENT[:], start=True, stop=True)
                            mm.then_inc(s_pe, 1)
                        E("tensor", f)

                for si, (off, wsl) in enumerate(slices):
                    cnt["ve"] += 1
                    ev = cnt["ve"]
                    for j in range(NT):
                        bank_war[4 * si + j] = ("ve", ev)
                    def f(e, si=si, off=off, wsl=wsl, i0=i0,
                          tm=tr_ms[(si, NT - 1)]):
                        e.wait_ge(s_pe, tm)
                        e.tensor_copy(out=SM[:, :, i0 + off:i0 + off + wsl],
                                      in_=PSA[:, 2048 * si:2048 * si + 2048]
                                      .rearrange("p (j w) -> p j w", w=512)[:, :, 0:wsl])
                        e.drain().then_inc(s_ve, 1)
                    E("vector", f)
            ms_b1evac_last[0] = cnt["ve"]

            if p + 1 < NPASS:
                emit_lnA(p + 1)

            for P in range(NOPAIR):
                oc = 1024 * P
                p_first_waits = [bank_war[4 * oo + j] for oo in range(2) for j in range(NT)]
                for ic in range(NIC + NHC):
                    didx[0] += 1
                    dbuf = didx[0] % NDBUF
                    war = dstr_war[dbuf]
                    dcnt[dbuf] += 1
                    dc = dcnt[dbuf]
                    def f(e, war=war, dbuf=dbuf, ic=ic, oc=oc):
                        if war:
                            e.wait_ge(s_pe, war)
                        e.dma_start(out=DSTR[:, dbuf, :],
                                    in_=dtd[128 * ic:128 * ic + 128, oc:oc + 1024]).then_inc(s_ds[dbuf], 16)
                    E("sync", f)

                    cnt["pe"] += 1
                    pe_ms = cnt["pe"]
                    dstr_war[dbuf] = pe_ms
                    def f(e, dc=dc, dbuf=dbuf, ic=ic,
                          first=(ic == 0), waits=p_first_waits,
                          extra=(ms_b1evac_last[0] if ic == 0 and P == 0 else 0),
                          qev=(ms_qevac[(p, NT - 1)] if ic == 0 and P == 0 else 0)):
                        e.wait_ge(s_ds[dbuf], 16 * dc)
                        if first:
                            if extra:
                                e.wait_ge(s_ve, extra)
                            if qev:
                                e.wait_ge(s_ve, qev)
                            for bw in waits:
                                e.wait_ge(SEM[bw[0]], bw[1])
                        for j in range(NT):
                            lhsT = (SM[:, j, 128 * ic:128 * ic + 128] if ic < NIC
                                    else XNF[:, ic - NIC, 128 * j:128 * j + 128])
                            for oo in range(2):
                                mm = e.matmul(PS[4 * oo + j][:, 0:512],
                                              lhsT,
                                              DSTR[:, dbuf, 512 * oo:512 * oo + 512],
                                              start=(ic == 0), stop=False,
                                              skip_group_check=True)
                        mm.then_inc(s_pe, 1)
                    E("tensor", f)

                grp_ms = {}
                for oo in range(2):
                    for j in range(NT):
                        cnt["pe"] += 1
                        grp_ms[(oo, j)] = cnt["pe"]
                        def f(e, oo=oo, j=j, oc=oc):
                            e.matmul(PS[4 * oo + j][:, 0:512], QF[:, 128 * j:128 * j + 128],
                                     BMTS[:, oc + 512 * oo:oc + 512 * oo + 512],
                                     start=False, stop=False, skip_group_check=True)
                            e.matmul(PS[4 * oo + j][:, 0:512], E0S[:],
                                     BPADS[:, oc + 512 * oo:oc + 512 * oo + 512],
                                     start=False, stop=True,
                                     skip_group_check=True).then_inc(s_pe, 1)
                        E("tensor", f)

                for oo in range(2):
                    o = 2 * P + oo
                    row0 = 512 * p
                    kbase = p * 32 + P * 8 + oo * 4
                    wv = ms_yevac.get(kbase - 1, 0)
                    xrcnt[0] += 1
                    rc = xrcnt[0]
                    def f(e, wv=wv, row0=row0, o=o):
                        if wv:
                            e.wait_ge(s_ve, wv)
                        e.dma_start(out=XRS[:],
                                    in_=xd[row0:row0 + 512, 512 * o:512 * o + 512]
                                    .rearrange("(j p) w -> p j w", p=128)).then_inc(s_xr[0], 16)
                    E("scalar", f)

                    for j in range(NT):
                        k = kbase + j
                        row = 512 * p + 128 * j
                        cnt["ve"] += 1
                        ms_yevac[k] = cnt["ve"]
                        bank_war[4 * oo + j] = ("ve", cnt["ve"])
                        ystw = yycnt[k % 2]
                        def f(e, oo=oo, j=j, k=k, rc=rc, gm=grp_ms[(oo, j)], ystw=ystw):
                            e.wait_ge(s_pe, gm)
                            e.wait_ge(s_xr[0], 16 * rc)
                            if ystw:
                                e.wait_ge(s_yy[k % 2], 16 * ystw)
                            e.tensor_tensor(out=YST[:, k % 2, :],
                                            in0=PS[4 * oo + j][:, 0:512],
                                            in1=XRS[:, j, :],
                                            op=ALU.add)
                            e.drain().then_inc(s_ve, 1)
                        E("vector", f)

                        yycnt[k % 2] += 1
                        def f(e, k=k, row=row, o=o, ym=ms_yevac[k]):
                            e.wait_ge(s_ve, ym)
                            e.dma_start(out=yd[row:row + 128, 512 * o:512 * o + 512],
                                        in_=YST[:, k % 2, :]).then_inc(s_yy[k % 2], 16)
                        E("gpsimd", f)

        def f_fin(e, t0=yycnt[0], t1=yycnt[1]):
            e.wait_ge(s_yy[0], 16 * t0)
            e.wait_ge(s_yy[1], 16 * t1)
        E("sync", f_fin)

        @blk.sync
        def _(e):
            for fn in plan["sync"]:
                fn(e)

        @blk.scalar
        def _(e):
            for fn in plan["scalar"]:
                fn(e)

        @blk.vector
        def _(e):
            for fn in plan["vector"]:
                fn(e)

        @blk.tensor
        def _(e):
            for fn in plan["tensor"]:
                fn(e)

        @blk.gpsimd
        def _(e):
            for fn in plan["gpsimd"]:
                fn(e)

    return nc


def kernel(L0_attn_x, L0_attn_w_gate, L0_attn_w_up, L0_attn_w_down,
           L0_ffn_W, L0_ffn_b, L0_ffn_A, L0_ffn_Bm, L0_ffn_Wr):
    x = np.ascontiguousarray(np.asarray(L0_attn_x, dtype=np.float32).reshape(NCORES * TCORE, H))
    gt = np.ascontiguousarray(np.asarray(L0_attn_w_gate).astype(BF).T)
    ut = np.ascontiguousarray(np.asarray(L0_attn_w_up).astype(BF).T)
    dt = np.concatenate([np.asarray(L0_attn_w_down).astype(BF).T,
                         np.asarray(L0_ffn_W).astype(BF).T], axis=0)
    dt = np.ascontiguousarray(dt)
    a2 = np.ascontiguousarray(np.asarray(L0_ffn_A).astype(BF).reshape(32, H).T
                              .reshape(NHC, 128, 32).transpose(1, 0, 2))
    wrt = np.ascontiguousarray(np.asarray(L0_ffn_Wr).astype(BF).T
                               .reshape(NHC, 128, 4).transpose(1, 0, 2))
    bmt = np.zeros((128, H), dtype=BF)
    bmt[0:32] = np.asarray(L0_ffn_Bm).astype(BF).transpose(0, 2, 1).reshape(32, H)
    bpad = np.zeros((128, H), dtype=BF)
    bpad[0] = np.asarray(L0_ffn_b).astype(BF)
    e0 = np.zeros((128, 128), dtype=BF)
    e0[0] = 1
    ident = np.eye(128, dtype=BF)

    nc = build_nc()
    shared = {"gt": gt, "ut": ut, "dt": dt, "a2": a2, "wrt": wrt,
              "bmt": bmt, "bpad": bpad, "e0": e0, "ident": ident}
    in_maps = [{"x": np.ascontiguousarray(x[c * TCORE:(c + 1) * TCORE]), **shared}
               for c in range(NCORES)]
    res = run_bass_kernel_spmd(nc, in_maps, core_ids=list(range(NCORES)))
    y = np.concatenate([r["y"] for r in res.results], axis=0)
    return y.reshape(2, 4096, 4096)



# version 29
# speedup vs baseline: 1.0360x; 1.0136x over previous
import numpy as np
import ml_dtypes
import concourse.bass as bass
import concourse.mybir as mybir
from concourse.bass_utils import run_bass_kernel_spmd

F32 = mybir.dt.float32
BF16 = mybir.dt.bfloat16
AF = mybir.ActivationFunctionType
ALU = mybir.AluOpType
AX = mybir.AxisListType
BF = ml_dtypes.bfloat16

NCORES = 8
H = 4096
II = 11008
TCORE = 1024
NPASS = 2
NT = 4
NHC = 32
NM = 11
MW = [1024] * 10 + [768]
NIC = 86
NOPAIR = 4
SUB_ITERS = 6
FULL_ITERS = 5
NWBUF = 8
NDBUF = 6

C_SX, C_SX2, C_M, C_V, C_RSTD, C_T1, C_TMID, C_NTM, C_NT2, C_DIF, C_S, C_MSK, \
    C_Q, C_D1, C_D2, C_THR, C_SC0, C_SC1, C_SC2, C_SC3, C_SC4, C_SC5, C_SC6, C_SC7 = range(24)
C_GM, C_GS, C_GR = 24, 25, 26
C_G4 = 28
C_GE4 = 32
C_MJ = 36
C_RJ = 40
C_TMJ = 44
C_NMJ = 48
NCOLS = 52
SUB_N = 1376
SEC_HW = 2.0 / 64 / 16
SEC_CL = 2.0 / 64 / 2


def build_nc():
    nc = bass.Bass("TRN2", target_bir_lowering=False, debug=False)
    xd = nc.dram_tensor("x", [TCORE, H], F32, kind="ExternalInput").ap()
    gtd = nc.dram_tensor("gt", [H, II], BF16, kind="ExternalInput").ap()
    utd = nc.dram_tensor("ut", [H, II], BF16, kind="ExternalInput").ap()
    dtd = nc.dram_tensor("dt", [II + H, H], BF16, kind="ExternalInput").ap()
    a2d = nc.dram_tensor("a2", [128, NHC, 32], BF16, kind="ExternalInput").ap()
    wrtd = nc.dram_tensor("wrt", [128, NHC, 4], BF16, kind="ExternalInput").ap()
    bmtd = nc.dram_tensor("bmt", [128, H], BF16, kind="ExternalInput").ap()
    bpd = nc.dram_tensor("bpad", [128, H], BF16, kind="ExternalInput").ap()
    e0d = nc.dram_tensor("e0", [128, 128], BF16, kind="ExternalInput").ap()
    idd = nc.dram_tensor("ident", [128, 128], BF16, kind="ExternalInput").ap()
    yd = nc.dram_tensor("y", [TCORE, H], F32, kind="ExternalOutput").ap()

    plan = {k: [] for k in ("sync", "scalar", "vector", "tensor", "gpsimd")}

    def E(eng, fn):
        plan[eng].append(fn)

    from contextlib import ExitStack
    with ExitStack() as ctx:
        SM = ctx.enter_context(nc.sbuf_tensor("SM", [128, NT, II], BF16))
        XNF = ctx.enter_context(nc.sbuf_tensor("XNF", [128, NHC, 512], BF16))
        ARN = ctx.enter_context(nc.sbuf_tensor("ARN", [128, 5504], F32))
        WSTR = ctx.enter_context(nc.sbuf_tensor("WSTR", [128, NWBUF, 1024], BF16))
        DSTR = ctx.enter_context(nc.sbuf_tensor("DSTR", [128, NDBUF, 1024], BF16))
        YST = ctx.enter_context(nc.sbuf_tensor("YST", [128, 2, 512], F32))
        XRS = ctx.enter_context(nc.sbuf_tensor("XRS", [128, 4, 512], F32))
        JNKB = ctx.enter_context(nc.sbuf_tensor("JNKB", [128, H], BF16))
        QF = ctx.enter_context(nc.sbuf_tensor("QF", [128, 512], BF16))
        QT = ctx.enter_context(nc.sbuf_tensor("QT", [128, 32], BF16))
        G2T = ctx.enter_context(nc.sbuf_tensor("G2T", [128, 4], BF16))
        A2S = ctx.enter_context(nc.sbuf_tensor("A2S", [128, NHC, 32], BF16))
        WRTS = ctx.enter_context(nc.sbuf_tensor("WRTS", [128, NHC, 4], BF16))
        BMTS = ctx.enter_context(nc.sbuf_tensor("BMTS", [128, H], BF16))
        BPADS = ctx.enter_context(nc.sbuf_tensor("BPADS", [128, H], BF16))
        E0S = ctx.enter_context(nc.sbuf_tensor("E0S", [128, 128], BF16))
        IDENT = ctx.enter_context(nc.sbuf_tensor("IDENT", [128, 128], BF16))
        COLS = ctx.enter_context(nc.sbuf_tensor("COLS", [128, NCOLS], F32))
        PSA = ctx.enter_context(nc.psum_tensor("PSA", [128, 4096], F32))
        PS = [PSA[:, 512 * i:512 * (i + 1)] for i in range(8)]
        s_pe = ctx.enter_context(nc.semaphore("s_pe"))
        s_ve = ctx.enter_context(nc.semaphore("s_ve"))
        s_se = ctx.enter_context(nc.semaphore("s_se"))
        s_ws = [ctx.enter_context(nc.semaphore(f"s_ws{i}")) for i in range(4)]
        s_wg = [ctx.enter_context(nc.semaphore(f"s_wg{i}")) for i in range(4)]
        s_wp = [ctx.enter_context(nc.semaphore(f"s_wp{i}")) for i in range(4)]
        s_ds = [ctx.enter_context(nc.semaphore(f"s_ds{i}")) for i in range(NDBUF)]
        s_xl = ctx.enter_context(nc.semaphore("s_xl"))
        s_xr = [ctx.enter_context(nc.semaphore(f"s_xr{i}")) for i in range(2)]
        s_yy = [ctx.enter_context(nc.semaphore(f"s_yy{i}")) for i in range(2)]
        s_cst = ctx.enter_context(nc.semaphore("s_cst"))
        blk = ctx.enter_context(nc.Block())
        SEM = {"pe": s_pe, "ve": s_ve, "se": s_se}
        ABSV = ARN[:].bitcast(BF16)[:, 0:II]
        XT = ARN[:, 0:H]

        cnt = {"pe": 0, "ve": 0, "se": 0}
        wcnt = [0] * NWBUF
        wpcnt = [0, 0, 0, 0]
        dcnt = [0] * NDBUF
        xlcnt = [0]
        xrcnt = [0, 0]
        yycnt = [0, 0]
        bank_war = [("ve", 0)] * 8
        wstr_war = [0] * NWBUF
        dstr_war = [0] * NDBUF
        widx = [0]
        didx = [0]
        ms_xnt = {}
        ms_mask = {}
        ms_smdone = {}
        ms_qevac = {}
        ms_lntrans = {}
        ms_yevac = {}
        ms_b1evac_last = [0]
        samp_ms = [0]

        def col(i, n=1):
            return COLS[:, i:i + n]

        NCST = 6
        def pre_sy(e):
            e.dma_start(out=IDENT[:], in_=idd).then_inc(s_cst, 16)
            e.dma_start(out=E0S[:], in_=e0d).then_inc(s_cst, 16)
            e.dma_start(out=A2S[:], in_=a2d).then_inc(s_cst, 16)
            e.dma_start(out=WRTS[:], in_=wrtd).then_inc(s_cst, 16)
            e.dma_start(out=BMTS[:], in_=bmtd).then_inc(s_cst, 16)
            e.dma_start(out=BPADS[:], in_=bpd).then_inc(s_cst, 16)
        E("sync", pre_sy)

        cnt["ve"] += 1
        def _pre_v(e):
            e.memset(QF[:], 0.0)
            e.drain().then_inc(s_ve, 1)
        E("vector", _pre_v)
        E("tensor", lambda e: e.wait_ge(s_cst, 16 * NCST))

        stats_ms = {}
        var_ms = {}
        rstd_ms = {}

        def emit_lnA(p):
            for j in range(NT):
                row = 512 * p + 128 * j
                if p == 0 and j == 0:
                    wv = None
                elif j == 0:
                    wv = ("ve", ms_mask[(p - 1, NT - 1)])
                else:
                    wv = ("se", stats_ms[(p, j - 1)])
                xlcnt[0] += 1
                xc = xlcnt[0]
                def f(e, wv=wv, row=row):
                    if wv:
                        e.wait_ge(SEM[wv[0]], wv[1])
                    e.dma_start(out=XT, in_=xd[row:row + 128, :]).then_inc(s_xl, 16)
                E("gpsimd", f)

                cnt["se"] += 1
                stats_ms[(p, j)] = cnt["se"]
                def f(e, xc=xc, vw=(var_ms.get((p, j - 1), 0))):
                    e.wait_ge(s_xl, 16 * xc)
                    if vw:
                        e.wait_ge(s_ve, vw)
                    e.activation(out=JNKB[:], in_=XT, func=AF.Identity,
                                 accum_out=col(C_SX))
                    e.drain()
                    e.activation(out=JNKB[:], in_=XT, func=AF.Square,
                                 accum_out=col(C_SX2))
                    e.drain().then_inc(s_se, 1)
                E("scalar", f)

                cnt["ve"] += 1
                var_ms[(p, j)] = cnt["ve"]
                def f(e, j=j, sm_=stats_ms[(p, j)]):
                    e.wait_ge(s_se, sm_)
                    e.tensor_scalar(out=col(C_MJ + j), in0=col(C_SX), scalar1=1.0 / H,
                                    scalar2=None, op0=ALU.mult)
                    e.tensor_scalar(out=col(C_V), in0=col(C_SX2), scalar1=1.0 / H,
                                    scalar2=None, op0=ALU.mult)
                    e.drain()
                    e.tensor_tensor(out=col(C_T1), in0=col(C_MJ + j),
                                    in1=col(C_MJ + j), op=ALU.mult)
                    e.drain()
                    e.tensor_tensor(out=col(C_V), in0=col(C_V), in1=col(C_T1),
                                    op=ALU.subtract)
                    e.drain()
                    e.tensor_scalar(out=col(C_V), in0=col(C_V), scalar1=1e-5,
                                    scalar2=None, op0=ALU.add)
                    e.drain().then_inc(s_ve, 1)
                E("vector", f)

                cnt["se"] += 1
                ln_ms = cnt["se"]
                def f(e, vm=var_ms[(p, j)]):
                    e.wait_ge(s_ve, vm)
                    e.activation(out=col(C_T1), in_=col(C_V), func=AF.Ln)
                    e.drain().then_inc(s_se, 1)
                E("scalar", f)
                cnt["ve"] += 1
                lnh_ms = cnt["ve"]
                def f(e, lm=ln_ms):
                    e.wait_ge(s_se, lm)
                    e.tensor_scalar(out=col(C_T1), in0=col(C_T1), scalar1=-0.5,
                                    scalar2=None, op0=ALU.mult)
                    e.drain().then_inc(s_ve, 1)
                E("vector", f)
                cnt["se"] += 1
                rstd_ms[(p, j)] = cnt["se"]
                def f(e, j=j, lh=lnh_ms):
                    e.wait_ge(s_ve, lh)
                    e.activation(out=col(C_RJ + j), in_=col(C_T1), func=AF.Exp)
                    e.drain().then_inc(s_se, 1)
                E("scalar", f)

        for p in range(NPASS):
            if p == 0:
                emit_lnA(0)
            for j in range(NT):
                row = 512 * p + 128 * j

                if j == 0:
                    wv = ("se", stats_ms[(p, NT - 1)])
                else:
                    wv = ("ve", ms_xnt[(p, j - 1)])
                xlcnt[0] += 1
                xc = xlcnt[0]
                def f(e, wv=wv, row=row):
                    e.wait_ge(SEM[wv[0]], wv[1])
                    e.dma_start(out=XT, in_=xd[row:row + 128, :]).then_inc(s_xl, 16)
                E("gpsimd", f)

                jw = ms_lntrans.get((p, j - 1), ms_lntrans.get((p - 1, NT - 1), 0))
                cnt["ve"] += 1
                ms_xnt[(p, j)] = cnt["ve"]
                def f(e, j=j, xc=xc, jw=jw, rm=rstd_ms[(p, j)]):
                    e.wait_ge(s_xl, 16 * xc)
                    e.wait_ge(s_se, rm)
                    if jw:
                        e.wait_ge(s_pe, jw)
                    e.tensor_scalar(out=JNKB[:], in0=XT, scalar1=col(C_MJ + j),
                                    scalar2=col(C_RJ + j), op0=ALU.subtract,
                                    op1=ALU.mult)
                    e.drain().then_inc(s_ve, 1)
                E("vector", f)

                for g8 in range(8):
                    bank = g8 % 2
                    bw = bank_war[bank]
                    cnt["pe"] += 1
                    grp_ms = cnt["pe"]
                    def f(e, g8=g8, bank=bank, bw=bw, j=j, first=(g8 == 0),
                          xnt=ms_xnt[(p, j)]):
                        if first:
                            e.wait_ge(s_ve, xnt)
                        e.wait_ge(SEM[bw[0]], bw[1])
                        for k in range(4):
                            hc = 4 * g8 + k
                            mm = e.matmul(PS[bank][:, 128 * k:128 * k + 128],
                                          JNKB[:, 128 * hc:128 * hc + 128], IDENT[:],
                                          start=True, stop=True)
                        mm.then_inc(s_pe, 1)
                    E("tensor", f)
                    cnt["ve"] += 1
                    ev_ms = cnt["ve"]
                    bank_war[bank] = ("ve", ev_ms)
                    def f(e, g8=g8, bank=bank, j=j, grp_ms=grp_ms):
                        e.wait_ge(s_pe, grp_ms)
                        e.tensor_copy(out=XNF[:, 4 * g8:4 * g8 + 4, 128 * j:128 * j + 128],
                                      in_=PS[bank][:, 0:512])
                        e.drain().then_inc(s_ve, 1)
                    E("vector", f)
                ms_lntrans[(p, j)] = cnt["pe"]
                xnf_done = cnt["ve"]

                bw2, bw3 = bank_war[2], bank_war[3]
                cnt["pe"] += 2
                gat_ms = cnt["pe"] - 1
                pt_ms = cnt["pe"]
                def f(e, j=j, bw2=bw2, bw3=bw3, xnf_done=xnf_done):
                    e.wait_ge(s_ve, xnf_done)
                    e.wait_ge(SEM[bw2[0]], bw2[1])
                    for hc in range(NHC):
                        mm = e.matmul(PS[2][:, 0:4], XNF[:, hc, 128 * j:128 * j + 128],
                                      WRTS[:, hc, :], start=(hc == 0), stop=(hc == NHC - 1),
                                      skip_group_check=True)
                    mm.then_inc(s_pe, 1)
                    e.wait_ge(SEM[bw3[0]], bw3[1])
                    for hc in range(NHC):
                        mm = e.matmul(PS[3][:, 0:32], XNF[:, hc, 128 * j:128 * j + 128],
                                      A2S[:, hc, :], start=(hc == 0), stop=(hc == NHC - 1),
                                      skip_group_check=True)
                    mm.then_inc(s_pe, 1)
                E("tensor", f)

                cnt["ve"] += 1
                gsub_ms = cnt["ve"]
                def f(e, gat_ms=gat_ms):
                    e.wait_ge(s_pe, gat_ms)
                    e.tensor_copy(out=col(C_G4, 4), in_=PS[2][:, 0:4])
                    e.drain()
                    e.reduce_max(out=col(C_GM), in_=col(C_G4, 4), axis=AX.X)
                    e.drain()
                    e.tensor_scalar(out=col(C_G4, 4), in0=col(C_G4, 4), scalar1=col(C_GM),
                                    scalar2=None, op0=ALU.subtract)
                    e.drain().then_inc(s_ve, 1)
                E("vector", f)
                bank_war[2] = ("ve", gsub_ms)

                cnt["se"] += 1
                gexp_ms = cnt["se"]
                def f(e, gsub_ms=gsub_ms):
                    e.wait_ge(s_ve, gsub_ms)
                    e.activation(out=col(C_GE4, 4), in_=col(C_G4, 4), func=AF.Exp)
                    e.drain().then_inc(s_se, 1)
                E("scalar", f)

                cnt["ve"] += 1
                qt_ms = cnt["ve"]
                bank_war[3] = ("ve", qt_ms)
                def f(e, gexp_ms=gexp_ms, pt_ms=pt_ms):
                    e.wait_ge(s_se, gexp_ms)
                    e.reduce_sum(out=col(C_GS), in_=col(C_GE4, 4), axis=AX.X)
                    e.drain()
                    e.reciprocal(out=col(C_GR), in_=col(C_GS))
                    e.drain()
                    e.tensor_scalar(out=G2T[:], in0=col(C_GE4, 4), scalar1=col(C_GR),
                                    scalar2=2.0, op0=ALU.mult, op1=ALU.mult)
                    e.drain()
                    e.wait_ge(s_pe, pt_ms)
                    e.tensor_tensor(out=QT[:].rearrange("p (e r) -> p e r", r=8),
                                    in0=PS[3][:, 0:32].rearrange("p (e r) -> p e r", r=8),
                                    in1=G2T[:].unsqueeze(2).broadcast_to([128, 4, 8]),
                                    op=ALU.mult)
                    e.drain().then_inc(s_ve, 1)
                E("vector", f)

                bw4 = bank_war[4]
                cnt["pe"] += 1
                qtr_ms = cnt["pe"]
                def f(e, j=j, bw4=bw4, qt_ms=qt_ms):
                    e.wait_ge(s_ve, qt_ms)
                    e.wait_ge(SEM[bw4[0]], bw4[1])
                    e.matmul(PS[4][0:32, 128 * j:128 * j + 128], QT[:], IDENT[:],
                             start=True, stop=True).then_inc(s_pe, 1)
                E("tensor", f)
                cnt["ve"] += 1
                ms_qevac[(p, j)] = cnt["ve"]
                bank_war[4] = ("ve", cnt["ve"])
                def f(e, j=j, qtr_ms=qtr_ms):
                    e.wait_ge(s_pe, qtr_ms)
                    e.tensor_copy(out=QF[0:32, 128 * j:128 * j + 128],
                                  in_=PS[4][0:32, 128 * j:128 * j + 128])
                    e.drain().then_inc(s_ve, 1)
                E("vector", f)

            for m in range(NM):
                w = MW[m]
                i0 = 1024 * m
                slices = [(0, 512), (512, w - 512)]
                m_first_waits = [bank_war[4 * si + j] for si in range(2) for j in range(NT)]
                pair_sem = {}
                for hc in range(NHC):
                    if hc % 2 == 0:
                        hc2 = hc // 2
                        widx[0] += 1
                        qgp = widx[0] % 2
                        b0 = 4 * qgp + 2 * ((widx[0] // 2) % 2)
                        sem = s_wg[b0 - 4] if qgp else s_ws[b0]
                        war = max(wstr_war[b0], wstr_war[b0 + 1])
                        wcnt[b0] += 1
                        gc = wcnt[b0]
                        pair_sem[hc2] = (sem, gc, b0)
                        def f(e, war=war, b0=b0, hcp=hc, i0=i0, w=w, sem=sem):
                            if war:
                                e.wait_ge(s_pe, war)
                            e.dma_start(out=WSTR[:, b0:b0 + 2, 0:w],
                                        in_=gtd[128 * hcp:128 * hcp + 256, i0:i0 + w]
                                        .rearrange("(c p) w -> p c w", p=128)).then_inc(sem, 16)
                        E("gpsimd" if qgp else "sync", f)
                    sem, gc, b0 = pair_sem[hc // 2]
                    buf = b0 + hc % 2
                    cnt["pe"] += 1
                    pe_ms = cnt["pe"]
                    wstr_war[buf] = pe_ms
                    def f(e, gc=gc, buf=buf, hc=hc, slices=slices, sem=sem,
                          first=(hc == 0), waits=m_first_waits):
                        e.wait_ge(sem, 16 * gc)
                        if first:
                            for bw in waits:
                                e.wait_ge(SEM[bw[0]], bw[1])
                        for si, (off, wsl) in enumerate(slices):
                            for j in range(NT):
                                mm = e.matmul(PS[4 * si + j][:, 0:wsl],
                                              XNF[:, hc, 128 * j:128 * j + 128],
                                              WSTR[:, buf, off:off + wsl],
                                              start=(hc == 0), stop=(hc == NHC - 1),
                                              skip_group_check=True)
                        mm.then_inc(s_pe, 1)
                    E("tensor", f)
                m_last_pe = cnt["pe"]

                for si, (off, wsl) in enumerate(slices):
                    cnt["se"] += 1
                    ev = cnt["se"]
                    for j in range(NT):
                        bank_war[4 * si + j] = ("se", ev)
                    def f(e, si=si, off=off, wsl=wsl, i0=i0,
                          m_last_pe=m_last_pe, first=(si == 0)):
                        if first:
                            e.wait_ge(s_pe, m_last_pe)
                        e.activation(out=SM[:, :, i0 + off:i0 + off + wsl],
                                     in_=PSA[:, 2048 * si:2048 * si + 2048]
                                     .rearrange("p (j w) -> p j w", w=512)[:, :, 0:wsl],
                                     func=AF.Silu)
                        e.drain().then_inc(s_se, 1)
                    E("scalar", f)
                if m == NM - 1:
                    for j in range(NT):
                        ms_smdone[(p, j)] = cnt["se"]

                DSAMP = DSTR[:].rearrange("p a b -> p (a b)")
                if m == 1:
                    cnt["ve"] += 1
                    samp_ms[0] = cnt["ve"]
                    def f(e, sds=[(i, dcnt[i]) for i in range(NDBUF)],
                          sev=cnt["se"]):
                        e.wait_ge(s_se, sev)
                        for i, c in sds:
                            if c:
                                e.wait_ge(s_ds[i], 16 * c)
                        for j in range(NT):
                            e.scalar_tensor_tensor(out=DSAMP[:, 1376 * j:1376 * (j + 1)],
                                                   in0=SM[:, j, 0:SUB_N], scalar=-1.0,
                                                   in1=SM[:, j, 0:SUB_N],
                                                   op0=ALU.mult, op1=ALU.max)
                            e.drain()
                        e.memset(col(C_TMJ, 4), 1.0)
                        e.memset(col(C_NMJ, 4), -1.0)
                        e.drain().then_inc(s_ve, 1)
                    E("vector", f)
                if 1 <= m <= SUB_ITERS:
                    it = m - 1
                    w_it = 2.0 / (2 ** (it + 1))
                    cnt["se"] += 1
                    s_it = cnt["se"]
                    def f(e, vw=(cnt["ve"] if it else samp_ms[0])):
                        e.wait_ge(s_ve, vw)
                        for j in range(NT):
                            if j:
                                e.drain()
                            e.activation(out=JNKB[:, 0:SUB_N],
                                         in_=DSAMP[:, 1376 * j:1376 * (j + 1)],
                                         func=AF.Sign, bias=col(C_NMJ + j),
                                         accum_out=col(C_SC0 + j))
                        e.drain().then_inc(s_se, 1)
                    E("scalar", f)
                    cnt["ve"] += 1
                    def f(e, s_it=s_it, w_it=w_it):
                        e.wait_ge(s_se, s_it)
                        e.tensor_single_scalar(out=col(C_SC4, 4), in_=col(C_SC0, 4),
                                               scalar=0.0, op=ALU.is_gt)
                        e.drain()
                        e.tensor_scalar(out=col(C_SC4, 4), in0=col(C_SC4, 4),
                                        scalar1=w_it, scalar2=-w_it / 2,
                                        op0=ALU.mult, op1=ALU.add)
                        e.drain()
                        e.tensor_tensor(out=col(C_TMJ, 4), in0=col(C_TMJ, 4),
                                        in1=col(C_SC4, 4), op=ALU.add)
                        e.drain()
                        e.tensor_scalar(out=col(C_NMJ, 4), in0=col(C_TMJ, 4),
                                        scalar1=-1.0, scalar2=None, op0=ALU.mult)
                        e.drain().then_inc(s_ve, 1)
                    E("vector", f)

            for j in range(NT):
                cnt["ve"] += 1
                abs_ms = cnt["ve"]
                def f(e, j=j, first=(j == 0), sd=max(ms_smdone[(p, jj)] for jj in range(NT))):
                    if first:
                        e.wait_ge(s_se, sd)
                    e.scalar_tensor_tensor(out=ABSV[:, 0:5504], in0=SM[:, j, 0:5504],
                                           scalar=-1.0, in1=SM[:, j, 0:5504],
                                           op0=ALU.mult, op1=ALU.max)
                    e.drain()
                    e.tensor_copy(out=col(C_TMID), in_=col(C_TMJ + j))
                    e.drain()
                    e.tensor_scalar(out=col(C_NTM), in0=col(C_TMID), scalar1=-1.0,
                                    scalar2=SEC_HW, op0=ALU.mult, op1=ALU.add)
                    e.drain()
                    e.tensor_scalar(out=col(C_NT2), in0=col(C_TMID), scalar1=-1.0,
                                    scalar2=-SEC_HW, op0=ALU.mult, op1=ALU.add)
                    e.drain()
                    e.tensor_scalar(out=col(C_Q), in0=col(C_TMID), scalar1=1.0,
                                    scalar2=SEC_HW, op0=ALU.mult, op1=ALU.add)
                    e.drain().then_inc(s_ve, 1)
                E("vector", f)
                cnt["ve"] += 1
                abs2_ms = cnt["ve"]
                def f(e, j=j):
                    e.scalar_tensor_tensor(out=ABSV[:, 5504:II], in0=SM[:, j, 5504:II],
                                           scalar=-1.0, in1=SM[:, j, 5504:II],
                                           op0=ALU.mult, op1=ALU.max)
                    e.drain().then_inc(s_ve, 1)
                E("vector", f)

                cnt["se"] += 1
                s_cnt = cnt["se"]
                def f(e, j=j, a1=abs_ms, a2=abs2_ms):
                    e.wait_ge(s_ve, a1)
                    for c in range(2):
                        if c:
                            e.drain()
                        e.activation(out=JNKB[:, 0:2752],
                                     in_=ABSV[:, 2752 * c:2752 * (c + 1)],
                                     func=AF.Sign, bias=col(C_NTM),
                                     accum_out=col(C_SC0 + c))
                    e.wait_ge(s_ve, a2)
                    for c in range(2, 4):
                        e.drain()
                        e.activation(out=JNKB[:, 0:2752],
                                     in_=ABSV[:, 2752 * c:2752 * (c + 1)],
                                     func=AF.Sign, bias=col(C_NTM),
                                     accum_out=col(C_SC0 + c))
                    e.drain()
                    e.activation(out=JNKB[:, 0:2752],
                                 in_=ABSV[:, 2752 * 3:2752 * 4],
                                 func=AF.Sign, bias=col(C_NT2),
                                 accum_out=col(C_SC7))
                    e.drain().then_inc(s_se, 1)
                E("scalar", f)

                cnt["ve"] += 1
                dve_cnt = cnt["ve"]
                def f(e, j=j, a2=abs2_ms):
                    for c in range(3):
                        if c:
                            e.drain()
                        e.tensor_scalar(out=DSAMP[:, 0:2752],
                                        in0=ABSV[:, 2752 * c:2752 * (c + 1)],
                                        scalar1=col(C_Q), scalar2=None,
                                        op0=ALU.is_gt)
                        e.drain()
                        e.tensor_reduce(out=col(C_SC4 + c), in_=DSAMP[:, 0:2752],
                                        axis=AX.X, op=ALU.add)
                    e.drain().then_inc(s_ve, 1)
                E("vector", f)

                cnt["ve"] += 1
                ms_mask[(p, j)] = cnt["ve"]
                def f(e, j=j, s_cnt=s_cnt):
                    e.wait_ge(s_se, s_cnt)
                    e.reduce_sum(out=col(C_S), in_=col(C_SC0, 4), axis=AX.X)
                    e.reduce_sum(out=col(C_D2), in_=col(C_SC4, 3), axis=AX.X)
                    e.drain()
                    e.tensor_scalar(out=col(C_D2), in0=col(C_D2), scalar1=2.0,
                                    scalar2=-3.0 * 2752, op0=ALU.mult, op1=ALU.add)
                    e.drain()
                    e.tensor_tensor(out=col(C_D2), in0=col(C_D2), in1=col(C_SC7),
                                    op=ALU.add)
                    e.drain()
                    e.tensor_tensor(out=col(C_DIF), in0=col(C_S), in1=col(C_D2),
                                    op=ALU.subtract)
                    e.drain()
                    e.tensor_scalar_max(col(C_DIF), col(C_DIF), 1.0)
                    e.drain()
                    e.reciprocal(out=col(C_DIF), in_=col(C_DIF))
                    e.drain()
                    e.tensor_tensor(out=col(C_Q), in0=col(C_S), in1=col(C_DIF),
                                    op=ALU.mult)
                    e.drain()
                    e.tensor_scalar(out=col(C_D1), in0=col(C_Q), scalar1=2 * SEC_HW,
                                    scalar2=-SEC_HW, op0=ALU.mult, op1=ALU.add)
                    e.drain()
                    e.tensor_scalar_min(col(C_D1), col(C_D1), SEC_CL)
                    e.drain()
                    e.tensor_scalar_max(col(C_D1), col(C_D1), -SEC_CL)
                    e.drain()
                    e.tensor_tensor(out=col(C_THR), in0=col(C_TMID), in1=col(C_D1),
                                    op=ALU.add)
                    e.drain()
                    e.scalar_tensor_tensor(out=SM[:, j, :], in0=ABSV, scalar=col(C_THR),
                                           in1=SM[:, j, :], op0=ALU.is_gt,
                                           op1=ALU.mult)
                    e.drain().then_inc(s_ve, 1)
                E("vector", f)

            for m in range(NM):
                w = MW[m]
                i0 = 1024 * m
                slices = [(0, 512), (512, w - 512)]
                m_first_waits = [bank_war[4 * si + j] for si in range(2) for j in range(NT)]
                pair_sem = {}
                for hc in range(NHC):
                    if hc % 2 == 0:
                        widx[0] += 1
                        qgp = widx[0] % 2
                        b0 = 4 * qgp + 2 * ((widx[0] // 2) % 2)
                        sem = s_wg[b0 - 4] if qgp else s_ws[b0]
                        war = max(wstr_war[b0], wstr_war[b0 + 1])
                        wcnt[b0] += 1
                        uc = wcnt[b0]
                        pair_sem[hc // 2] = (sem, uc, b0)
                        def f(e, war=war, b0=b0, hcp=hc, i0=i0, w=w, sem=sem):
                            if war:
                                e.wait_ge(s_pe, war)
                            e.dma_start(out=WSTR[:, b0:b0 + 2, 0:w],
                                        in_=utd[128 * hcp:128 * hcp + 256, i0:i0 + w]
                                        .rearrange("(c p) w -> p c w", p=128)).then_inc(sem, 16)
                        E("gpsimd" if qgp else "sync", f)
                    sem, uc, b0 = pair_sem[hc // 2]
                    buf = b0 + hc % 2
                    cnt["pe"] += 1
                    pe_ms = cnt["pe"]
                    wstr_war[buf] = pe_ms
                    def f(e, uc=uc, buf=buf, hc=hc, slices=slices,
                          first=(hc == 0), waits=m_first_waits,
                          sem=sem):
                        e.wait_ge(sem, 16 * uc)
                        if first:
                            for bw in waits:
                                e.wait_ge(SEM[bw[0]], bw[1])
                        for j in range(NT):
                            for si, (off, wsl) in enumerate(slices):
                                mm = e.matmul(PS[4 * si + j][:, 0:wsl],
                                              XNF[:, hc, 128 * j:128 * j + 128],
                                              WSTR[:, buf, off:off + wsl],
                                              start=(hc == 0), stop=(hc == NHC - 1),
                                              skip_group_check=True)
                        mm.then_inc(s_pe, 1)
                    E("tensor", f)
                m_last_pe = cnt["pe"]

                prod_ms = {}
                for si, (off, wsl) in enumerate(slices):
                    cnt["ve"] += 1
                    prod_ms[si] = cnt["ve"]
                    def f(e, si=si, off=off, wsl=wsl, i0=i0,
                          m_last_pe=m_last_pe, first=(si == 0)):
                        if first:
                            e.wait_ge(s_pe, m_last_pe)
                        e.tensor_tensor(out=SM[:, :, i0 + off:i0 + off + wsl],
                                        in0=SM[:, :, i0 + off:i0 + off + wsl],
                                        in1=PSA[:, 2048 * si:2048 * si + 2048]
                                        .rearrange("p (j w) -> p j w", w=512)[:, :, 0:wsl],
                                        op=ALU.mult)
                        e.drain().then_inc(s_ve, 1)
                    E("vector", f)

                tr_ms = {}
                for si, (off, wsl) in enumerate(slices):
                    for j in range(NT):
                        cnt["pe"] += 1
                        tr_ms[(si, j)] = cnt["pe"]
                        def f(e, j=j, si=si, off=off, wsl=wsl, i0=i0,
                              pm=prod_ms[si]):
                            e.wait_ge(s_ve, pm)
                            for c in range(wsl // 128):
                                mm = e.matmul(PS[4 * si + j][:, 128 * c:128 * c + 128],
                                              SM[:, j, i0 + off + 128 * c:i0 + off + 128 * c + 128],
                                              IDENT[:], start=True, stop=True)
                            mm.then_inc(s_pe, 1)
                        E("tensor", f)

                for si, (off, wsl) in enumerate(slices):
                    cnt["ve"] += 1
                    ev = cnt["ve"]
                    for j in range(NT):
                        bank_war[4 * si + j] = ("ve", ev)
                    def f(e, si=si, off=off, wsl=wsl, i0=i0,
                          tm=tr_ms[(si, NT - 1)]):
                        e.wait_ge(s_pe, tm)
                        e.tensor_copy(out=SM[:, :, i0 + off:i0 + off + wsl],
                                      in_=PSA[:, 2048 * si:2048 * si + 2048]
                                      .rearrange("p (j w) -> p j w", w=512)[:, :, 0:wsl])
                        e.drain().then_inc(s_ve, 1)
                    E("vector", f)
            ms_b1evac_last[0] = cnt["ve"]

            if p + 1 < NPASS:
                emit_lnA(p + 1)

            for P in range(NOPAIR):
                oc = 1024 * P
                p_first_waits = [bank_war[4 * oo + j] for oo in range(2) for j in range(NT)]
                for ic in range(NIC + NHC):
                    didx[0] += 1
                    dbuf = didx[0] % NDBUF
                    war = dstr_war[dbuf]
                    dcnt[dbuf] += 1
                    dc = dcnt[dbuf]
                    def f(e, war=war, dbuf=dbuf, ic=ic, oc=oc):
                        if war:
                            e.wait_ge(s_pe, war)
                        e.dma_start(out=DSTR[:, dbuf, :],
                                    in_=dtd[128 * ic:128 * ic + 128, oc:oc + 1024]).then_inc(s_ds[dbuf], 16)
                    E("sync", f)

                    cnt["pe"] += 1
                    pe_ms = cnt["pe"]
                    dstr_war[dbuf] = pe_ms
                    def f(e, dc=dc, dbuf=dbuf, ic=ic,
                          first=(ic == 0), waits=p_first_waits,
                          extra=(ms_b1evac_last[0] if ic == 0 and P == 0 else 0),
                          qev=(ms_qevac[(p, NT - 1)] if ic == 0 and P == 0 else 0)):
                        e.wait_ge(s_ds[dbuf], 16 * dc)
                        if first:
                            if extra:
                                e.wait_ge(s_ve, extra)
                            if qev:
                                e.wait_ge(s_ve, qev)
                            for bw in waits:
                                e.wait_ge(SEM[bw[0]], bw[1])
                        for j in range(NT):
                            lhsT = (SM[:, j, 128 * ic:128 * ic + 128] if ic < NIC
                                    else XNF[:, ic - NIC, 128 * j:128 * j + 128])
                            for oo in range(2):
                                mm = e.matmul(PS[4 * oo + j][:, 0:512],
                                              lhsT,
                                              DSTR[:, dbuf, 512 * oo:512 * oo + 512],
                                              start=(ic == 0), stop=False,
                                              skip_group_check=True)
                        mm.then_inc(s_pe, 1)
                    E("tensor", f)

                grp_ms = {}
                for oo in range(2):
                    for j in range(NT):
                        cnt["pe"] += 1
                        grp_ms[(oo, j)] = cnt["pe"]
                        def f(e, oo=oo, j=j, oc=oc):
                            e.matmul(PS[4 * oo + j][:, 0:512], QF[:, 128 * j:128 * j + 128],
                                     BMTS[:, oc + 512 * oo:oc + 512 * oo + 512],
                                     start=False, stop=False, skip_group_check=True)
                            e.matmul(PS[4 * oo + j][:, 0:512], E0S[:],
                                     BPADS[:, oc + 512 * oo:oc + 512 * oo + 512],
                                     start=False, stop=True,
                                     skip_group_check=True).then_inc(s_pe, 1)
                        E("tensor", f)

                for oo in range(2):
                    o = 2 * P + oo
                    row0 = 512 * p
                    kbase = p * 32 + P * 8 + oo * 4
                    wv = ms_yevac.get(kbase - 1, 0)
                    xrcnt[0] += 1
                    rc = xrcnt[0]
                    def f(e, wv=wv, row0=row0, o=o):
                        if wv:
                            e.wait_ge(s_ve, wv)
                        e.dma_start(out=XRS[:],
                                    in_=xd[row0:row0 + 512, 512 * o:512 * o + 512]
                                    .rearrange("(j p) w -> p j w", p=128)).then_inc(s_xr[0], 16)
                    E("scalar", f)

                    for j in range(NT):
                        k = kbase + j
                        row = 512 * p + 128 * j
                        cnt["ve"] += 1
                        ms_yevac[k] = cnt["ve"]
                        bank_war[4 * oo + j] = ("ve", cnt["ve"])
                        ystw = yycnt[k % 2]
                        def f(e, oo=oo, j=j, k=k, rc=rc, gm=grp_ms[(oo, j)], ystw=ystw):
                            e.wait_ge(s_pe, gm)
                            e.wait_ge(s_xr[0], 16 * rc)
                            if ystw:
                                e.wait_ge(s_yy[k % 2], 16 * ystw)
                            e.tensor_tensor(out=YST[:, k % 2, :],
                                            in0=PS[4 * oo + j][:, 0:512],
                                            in1=XRS[:, j, :],
                                            op=ALU.add)
                            e.drain().then_inc(s_ve, 1)
                        E("vector", f)

                        yycnt[k % 2] += 1
                        def f(e, k=k, row=row, o=o, ym=ms_yevac[k]):
                            e.wait_ge(s_ve, ym)
                            e.dma_start(out=yd[row:row + 128, 512 * o:512 * o + 512],
                                        in_=YST[:, k % 2, :]).then_inc(s_yy[k % 2], 16)
                        E("gpsimd", f)

        def f_fin(e, t0=yycnt[0], t1=yycnt[1]):
            e.wait_ge(s_yy[0], 16 * t0)
            e.wait_ge(s_yy[1], 16 * t1)
        E("sync", f_fin)

        @blk.sync
        def _(e):
            for fn in plan["sync"]:
                fn(e)

        @blk.scalar
        def _(e):
            for fn in plan["scalar"]:
                fn(e)

        @blk.vector
        def _(e):
            for fn in plan["vector"]:
                fn(e)

        @blk.tensor
        def _(e):
            for fn in plan["tensor"]:
                fn(e)

        @blk.gpsimd
        def _(e):
            for fn in plan["gpsimd"]:
                fn(e)

    return nc


def kernel(L0_attn_x, L0_attn_w_gate, L0_attn_w_up, L0_attn_w_down,
           L0_ffn_W, L0_ffn_b, L0_ffn_A, L0_ffn_Bm, L0_ffn_Wr):
    x = np.ascontiguousarray(np.asarray(L0_attn_x, dtype=np.float32).reshape(NCORES * TCORE, H))
    gt = np.ascontiguousarray(np.asarray(L0_attn_w_gate).astype(BF).T)
    ut = np.ascontiguousarray(np.asarray(L0_attn_w_up).astype(BF).T)
    dt = np.concatenate([np.asarray(L0_attn_w_down).astype(BF).T,
                         np.asarray(L0_ffn_W).astype(BF).T], axis=0)
    dt = np.ascontiguousarray(dt)
    a2 = np.ascontiguousarray(np.asarray(L0_ffn_A).astype(BF).reshape(32, H).T
                              .reshape(NHC, 128, 32).transpose(1, 0, 2))
    wrt = np.ascontiguousarray(np.asarray(L0_ffn_Wr).astype(BF).T
                               .reshape(NHC, 128, 4).transpose(1, 0, 2))
    bmt = np.zeros((128, H), dtype=BF)
    bmt[0:32] = np.asarray(L0_ffn_Bm).astype(BF).transpose(0, 2, 1).reshape(32, H)
    bpad = np.zeros((128, H), dtype=BF)
    bpad[0] = np.asarray(L0_ffn_b).astype(BF)
    e0 = np.zeros((128, 128), dtype=BF)
    e0[0] = 1
    ident = np.eye(128, dtype=BF)

    nc = build_nc()
    shared = {"gt": gt, "ut": ut, "dt": dt, "a2": a2, "wrt": wrt,
              "bmt": bmt, "bpad": bpad, "e0": e0, "ident": ident}
    in_maps = [{"x": np.ascontiguousarray(x[c * TCORE:(c + 1) * TCORE]), **shared}
               for c in range(NCORES)]
    res = run_bass_kernel_spmd(nc, in_maps, core_ids=list(range(NCORES)))
    y = np.concatenate([r["y"] for r in res.results], axis=0)
    return y.reshape(2, 4096, 4096)


